# version 12 (final)
# baseline (speedup 1.0000x reference)
import math
import contextlib
import numpy as np
import concourse.bass as bass
import concourse.mybir as mybir
from concourse.bass_utils import run_bass_kernel_spmd

F32 = mybir.dt.float32
BF16 = mybir.dt.bfloat16
AF = mybir.ActivationFunctionType
ALU = mybir.AluOpType
AX = mybir.AxisListType

D = 2048
HEADS = 16
NOPE = 128
ROPE = 64
VD = 128
QH = NOPE + ROPE
KVR = 512
NQ = 1024
NK = 2048
NE = 8
INTER = 1024
C_ATT = 1.0 / math.sqrt(QH)
EPS = 1e-5
KCJ = [1, 1, 2, 2, 3, 3, 4, 4]
QSETS = [[0, 3, 4, 7, 8, 11, 12, 15], [1, 2, 5, 6, 9, 10, 13, 14]]
NEG = -1.0e30


class Ctr:
    def __init__(self, sem):
        self.sem = sem
        self.n = 0

    def inc(self, k=1):
        self.n += k
        return (self.sem, k)

    def at(self):
        return (self.sem, self.n)


PHASES = 7


def build_program(phases=None):
    if phases is None:
        phases = PHASES
    nc = bass.Bass("TRN2", target_bir_lowering=False)

    xq = nc.dram_tensor("xq", [NQ, D], F32, kind="ExternalInput").ap()
    xkv = nc.dram_tensor("xkv", [NK, D], F32, kind="ExternalInput").ap()
    wqa = nc.dram_tensor("wqa", [D, KVR], F32, kind="ExternalInput").ap()
    wqb = nc.dram_tensor("wqb", [KVR, HEADS * QH], F32, kind="ExternalInput").ap()
    wkva = nc.dram_tensor("wkva", [D, KVR + ROPE], F32, kind="ExternalInput").ap()
    wkvb = nc.dram_tensor("wkvb", [KVR, HEADS * (NOPE + VD)], F32, kind="ExternalInput").ap()
    wout = nc.dram_tensor("wout", [HEADS * VD, D], F32, kind="ExternalInput").ap()
    rgate = nc.dram_tensor("rgate", [NE, D], F32, kind="ExternalInput").ap()
    wgu = nc.dram_tensor("wgu", [NE, D, 2 * INTER], F32, kind="ExternalInput").ap()
    wdn = nc.dram_tensor("wdn", [NE, INTER, D], F32, kind="ExternalInput").ap()
    maskin = nc.dram_tensor("maskin", [128, 8 * 512], F32, kind="ExternalInput").ap()
    csq = nc.dram_tensor("csq", [128, NQ], F32, kind="ExternalInput").ap()
    csk = nc.dram_tensor("csk", [128, NK], F32, kind="ExternalInput").ap()
    identin = nc.dram_tensor("identin", [128, 128], F32, kind="ExternalInput").ap()
    swpin = nc.dram_tensor("swpin", [64, 64], F32, kind="ExternalInput").ap()
    onesrow = nc.dram_tensor("onesrow", [1, 128], F32, kind="ExternalInput").ap()
    onescol = nc.dram_tensor("onescol", [128, 1], F32, kind="ExternalInput").ap()
    epscol = nc.dram_tensor("epscol", [128, 1], F32, kind="ExternalInput").ap()
    out = nc.dram_tensor("out", [NQ, D], F32, kind="ExternalOutput").ap()

    es = contextlib.ExitStack()

    def sbuf(name, shape, dt):
        return es.enter_context(nc.sbuf_tensor(name, shape, dt))

    XFM = sbuf("XFM", [128, 16 * 1024], F32)
    MQ = sbuf("MQ", [128, 8 * 512], BF16)
    IDF = sbuf("IDF", [128, 128], F32)
    IDB = sbuf("IDB", [128, 128], BF16)
    SWPF = sbuf("SWPF", [64, 64], F32)
    ONEB = sbuf("ONEB", [128, 1], BF16)
    ONEFC = sbuf("ONEFC", [128, 1], F32)
    ONEF = sbuf("ONEF", [1, 128], F32)
    SM = sbuf("SM", [128, 32], F32)
    OTOK = sbuf("OTOK", [128, 128], BF16)
    EPSC = sbuf("EPSC", [128, 1], F32)
    NM_, S_, AL_, DD_, T0_, T1_, T2_, T3_ = 0, 1, 2, 3, 4, 5, 6, 7

    def smv(c):
        return SM[:, c:c + 1]

    ABW = 49152
    AFW = 8448
    AB = sbuf("AB", [128, ABW], BF16)
    AFA = sbuf("AFA", [128, AFW], F32)

    class Arena:
        def __init__(self, t, width):
            self.t, self.w, self.off = t, width, 0

        def alloc(self, cols, parts=128):
            assert self.off + cols <= self.w, f"arena overflow {self.off}+{cols}>{self.w}"
            v = self.t[0:parts, self.off:self.off + cols]
            self.off += cols
            return v

        def mark(self):
            return self.off

        def reset(self, off=0):
            self.off = off

    ab = Arena(AB, ABW)
    af = Arena(AFA, AFW)

    qa_b = ab.alloc(4 * 1024)
    comp_b = ab.alloc(4 * 2048)
    kpe_b = ab.alloc(2048, parts=64)
    AB_P01 = ab.mark()
    csq_s = af.alloc(1024)
    AF_P01 = af.mark()

    PS = es.enter_context(nc.psum_tensor("PS", [128, 4096], F32))

    def psf(bank, off=0, w=512, parts=128):
        return PS[0:parts, bank * 512 + off:bank * 512 + off + w]

    def psb(bank, off_w=0, w_bf=512, parts=128):
        return PS[0:parts, bank * 512 + off_w:bank * 512 + off_w + w_bf // 2].bitcast(BF16)

    sQ = Ctr(es.enter_context(nc.semaphore("sQ")))
    sG = Ctr(es.enter_context(nc.semaphore("sG")))
    sP = Ctr(es.enter_context(nc.semaphore("sP")))
    sV = Ctr(es.enter_context(nc.semaphore("sV")))
    sS = Ctr(es.enter_context(nc.semaphore("sS")))
    sO = Ctr(es.enter_context(nc.semaphore("sO")))

    prog = {k: [] for k in ("sync", "gpsimd", "tensor", "vector", "scalar")}

    def op(eng, name, *args, waits=(), inc=None, **kw):
        waits = tuple(w for w in waits if w is not None)

        def f(e, name=name, args=args, kw=kw, waits=waits, inc=inc):
            for (sem, val) in waits:
                if val > 0:
                    e.wait_ge(sem, val)
            inst = getattr(e, name)(*args, **kw)
            if inc is not None:
                inst.then_inc(*inc)
            return inst
        prog[eng].append(f)
        if name == "dma_start" and inc is not None and inc[0] in (sQ.sem, sG.sem, sO.sem):
            ctr = sQ if inc[0] == sQ.sem else (sG if inc[0] == sG.sem else sO)
            waitonly(eng, ctr.at())

    def waitonly(eng, *waits):
        waits = tuple(waits)

        def f(e, waits=waits):
            for (sem, val) in waits:
                e.wait_ge(sem, val)
        prog[eng].append(f)

    def _emit_all():
        with nc.Block() as block:
            @block.sync
            def _(e):
                for f in prog["sync"]:
                    f(e)

            @block.gpsimd
            def _(e):
                for f in prog["gpsimd"]:
                    f(e)

            @block.tensor
            def _(e):
                for f in prog["tensor"]:
                    f(e)

            @block.vector
            def _(e):
                for f in prog["vector"]:
                    f(e)

            @block.scalar
            def _(e):
                for f in prog["scalar"]:
                    f(e)

    bank_free = {}

    def _finish_debug():
        dbgab = nc.dram_tensor("dbgab", [128, ABW], BF16, kind="ExternalOutput").ap()
        dbgaf = nc.dram_tensor("dbgaf", [128, AFW], F32, kind="ExternalOutput").ap()
        for j in range(8):
            op("sync", "dma_start", out[j * 128:(j + 1) * 128, :],
               XFM[:, j * 2048:(j + 1) * 2048],
               waits=([sP.at(), sV.at(), sS.at(), sG.at(), sQ.at()] if j == 0 else []),
               inc=sO.inc(16))
        dbgsm = nc.dram_tensor("dbgsm", [128, 32], F32, kind="ExternalOutput").ap()
        dbgot = nc.dram_tensor("dbgot", [128, 128], BF16, kind="ExternalOutput").ap()
        op("sync", "dma_start", dbgot, OTOK[:], inc=sO.inc(16))
        op("sync", "dma_start", dbgab, AB[:], inc=sO.inc(16))
        op("sync", "dma_start", dbgaf, AFA[:], inc=sO.inc(16))
        op("sync", "dma_start", dbgsm, SM[:], inc=sO.inc(16))
        waitonly("sync", sO.at())
        _emit_all()
        es.close()
        return nc


    def bfw(key):
        return bank_free.get(key)

    def chunked(dst2d, src2d, nk):
        c = src2d.shape[-1]
        return (dst2d.rearrange("p (k c) -> p k c", k=nk),
                src2d.rearrange("(k p) c -> p k c", p=128))

    op("sync", "dma_start", IDF[:], identin, inc=sQ.inc(16))
    op("sync", "dma_start", SWPF[:], swpin, inc=sQ.inc(16))
    op("sync", "dma_start", csq_s, csq, inc=sQ.inc(16))
    consts_q = sQ.at()
    op("gpsimd", "dma_start", MQ[:], maskin, inc=sG.inc(16))
    op("gpsimd", "dma_start", IDB[:], identin, inc=sG.inc(16))
    consts_g = sG.at()
    op("sync", "dma_start", ONEF[:], onesrow, inc=sQ.inc(16))
    op("sync", "dma_start", ONEFC[:], onescol, inc=sQ.inc(16))
    op("gpsimd", "dma_start", ONEB[:], onescol, inc=sG.inc(16))
    op("sync", "dma_start", EPSC[:], epscol, inc=sQ.inc(16))
    ones_q = sQ.at()
    ones_g = sG.at()

    if phases <= 0.5:
        return _finish_debug()

    p0_ab = ab.mark()
    p0_af = af.mark()
    xqb = ab.alloc(16 * 1024)
    wqa_b = ab.alloc(16 * 512)
    stagf = [af.alloc(2048) for _ in range(2)]

    op("gpsimd", "dma_start", *chunked(wqa_b, wqa, 16), inc=sG.inc(16))
    wqa_g = sG.at()

    tile_tp_done = {}
    for j in range(8):
        w = [] if j < 2 else [tile_tp_done[("xq", j - 2)]]
        op("sync", "dma_start", stagf[j % 2], xq[j * 128:(j + 1) * 128, :],
           waits=w, inc=sQ.inc(16))
        tload = sQ.at()
        XFM3 = XFM.rearrange("p (k t) -> p k t", t=1024)
        for cg in range(4):
            bank = cg % 2
            for ci in range(4):
                c = cg * 4 + ci
                pv = psf(bank, ci * 128, 128)
                w = []
                if ci == 0:
                    if cg == 0:
                        w = [tload, consts_q]
                    if bfw(("tpqV", bank)):
                        w.append(bfw(("tpqV", bank)))
                op("tensor", "transpose", pv, stagf[j % 2][:, c * 128:(c + 1) * 128],
                   IDF[:], waits=w, inc=(sP.inc() if ci == 3 else None))
            op("vector", "tensor_copy",
               XFM3[:, cg * 4:(cg + 1) * 4, j * 128:(j + 1) * 128],
               psf(bank).rearrange("p (k t) -> p k t", t=128),
               waits=[sP.at()], inc=sV.inc())
            bank_free[("tpqV", bank)] = sV.at()
        op("scalar", "copy",
           xqb.rearrange("p (k t) -> p k t", t=1024)[:, :, j * 128:(j + 1) * 128],
           XFM3[:, :, j * 128:(j + 1) * 128], waits=[sV.at()], inc=sS.inc())
        tile_tp_done[("xq", j)] = sP.at()
    xq_done = sS.at()

    if phases <= 1:
        return _finish_debug()

    for m in range(4):
        for n in range(2):
            bank = 2 + n
            for k in range(16):
                w = []
                if k == 0:
                    w = [xq_done]
                    if m == 0 and n == 0:
                        w.append(wqa_g)
                    if bfw(("qa", bank)):
                        w.append(bfw(("qa", bank)))
                op("tensor", "matmul", psf(bank),
                   wqa_b[:, k * 512 + m * 128:k * 512 + (m + 1) * 128],
                   xqb[:, k * 1024 + n * 512:k * 1024 + (n + 1) * 512],
                   start=(k == 0), stop=(k == 15), waits=w,
                   inc=(sP.inc() if k == 15 else None))
            op("scalar", "copy", qa_b[:, m * 1024 + n * 512:m * 1024 + (n + 1) * 512],
               psf(bank), waits=[sP.at()], inc=sS.inc())
            bank_free[("qa", bank)] = sS.at()
    qa_done = sS.at()
    phase0c_pe = sP.at()

    if phases <= 1.5:
        return _finish_debug()

    ab.reset(p0_ab)
    af.reset(p0_af)
    stagb = [ab.alloc(2048) for _ in range(2)]
    grp_b = ab.alloc(16 * 256)
    wkva_b = ab.alloc(16 * 576)
    csk_s = af.alloc(2048)
    kpe_f = af.alloc(2048, parts=64)
    ktmp = af.alloc(2048, parts=64)

    op("sync", "dma_start", csk_s, csk, waits=[tile_tp_done[("xq", 7)]], inc=sQ.inc(16))
    csk_q = sQ.at()
    op("gpsimd", "dma_start", *chunked(wkva_b, wkva, 16),
       waits=[phase0c_pe], inc=sG.inc(16))
    wkva_g = sG.at()

    last_comp_evac = None
    last_kpe_evac = None
    for g in range(8):
        for ti in range(2):
            t = 2 * g + ti
            w = [phase0c_pe, consts_g] if t == 0 else []
            if t >= 2:
                w.append(tile_tp_done[t - 2])
            op("gpsimd", "dma_start", stagb[t % 2], xkv[t * 128:(t + 1) * 128, :],
               waits=w, inc=sG.inc(16))
            tload = sG.at()
            grp3 = grp_b.rearrange("p (k t) -> p k t", t=256)
            for cg in range(2):
                bank = cg
                for ci in range(8):
                    c = cg * 8 + ci
                    pv = psb(bank, ci * 64, 128)
                    w2 = []
                    if ci == 0:
                        if cg == 0:
                            w2 = [tload]
                        if t == 0:
                            w2.append(bfw(("tpqV", bank)))
                        if bfw(("tp", bank)):
                            w2.append(bfw(("tp", bank)))
                    op("tensor", "transpose", pv, stagb[t % 2][:, c * 128:(c + 1) * 128],
                       IDB[:], waits=w2, inc=(sP.inc() if ci == 7 else None))
                op("scalar", "copy",
                   grp3[:, cg * 8:(cg + 1) * 8, ti * 128:(ti + 1) * 128],
                   psb(bank, 0, 1024).rearrange("p (k t) -> p k t", t=128),
                   waits=[sP.at()], inc=sS.inc())
                bank_free[("tp", bank)] = sS.at()
            tile_tp_done[t] = sP.at()
        grp_done = sS.at()
        for m in range(5):
            parts = 128 if m < 4 else 64
            mw = 128 if m < 4 else 64
            bank = 2 + m
            for k in range(16):
                w = []
                if k == 0:
                    w = [grp_done]
                    if g == 0 and m == 0:
                        w.append(wkva_g)
                    if g == 0 and m < 2:
                        w.append(bfw(("qa", bank)))
                    if bfw(("ckv", bank)):
                        w.append(bfw(("ckv", bank)))
                op("tensor", "matmul", psf(bank, 0, 256, parts),
                   wkva_b[:, k * 576 + m * 128: k * 576 + m * 128 + mw],
                   grp_b[:, k * 256:(k + 1) * 256],
                   start=(k == 0), stop=(k == 15), waits=w,
                   inc=(sP.inc() if k == 15 else None))
            if m < 4:
                op("scalar", "copy",
                   comp_b[:, m * 2048 + g * 256: m * 2048 + (g + 1) * 256],
                   psf(bank, 0, 256), waits=[sP.at()], inc=sS.inc())
                bank_free[("ckv", bank)] = sS.at()
                last_comp_evac = sS.at()
            else:
                op("vector", "tensor_copy", kpe_f[:, g * 256:(g + 1) * 256],
                   psf(bank, 0, 256, 64), waits=[sP.at()], inc=sV.inc())
                bank_free[("ckv", bank)] = sV.at()
                last_kpe_evac = sV.at()

    swp_w = [last_kpe_evac, consts_q, last_comp_evac]
    for i in range(4):
        w = list(swp_w) if i == 0 else []
        if i >= 2 and bfw(("ckv", i)):
            w.append(bfw(("ckv", i)))
        op("tensor", "matmul", psf(i, 0, 512, 64), SWPF[:], kpe_f[:, i * 512:(i + 1) * 512],
           start=True, stop=True, waits=w, inc=(sP.inc() if i == 3 else None))
    swp_pe = sP.at()
    op("vector", "tensor_mul", kpe_f, kpe_f, csk_s[0:64, :], waits=[swp_pe, csk_q])
    for i in range(4):
        op("vector", "tensor_mul", ktmp[:, i * 512:(i + 1) * 512],
           psf(i, 0, 512, 64), csk_s[64:128, i * 512:(i + 1) * 512])
    for i in range(4):
        op("vector", "tensor_add", kpe_b[:, i * 512:(i + 1) * 512],
           kpe_f[:, i * 512:(i + 1) * 512], ktmp[:, i * 512:(i + 1) * 512],
           inc=(sV.inc() if i == 3 else None))
    kpe_done = sV.at()
    phase0_pe = sP.at()

    if phases <= 2:
        return _finish_debug()

    ab.reset(AB_P01)
    af.reset(AF_P01)
    kf2 = ab.alloc(2 * 2048)
    v2 = ab.alloc(16 * 256)
    qn2 = ab.alloc(2 * 1024)
    qp2 = ab.alloc(2 * 1024, parts=64)
    psb_ = [ab.alloc(512) for _ in range(4)]
    ptb_ = [ab.alloc(512) for _ in range(4)]
    ofm = ab.alloc(2 * 1024)
    wqb_b = [ab.alloc(4 * 384) for _ in range(2)]
    wkvb_b = [ab.alloc(4 * 512) for _ in range(2)]
    wout_b = [ab.alloc(2 * 2048) for _ in range(2)]
    qpe_f = af.alloc(1024, parts=64)
    qtmp = af.alloc(1024, parts=64)
    oacc = af.alloc(256)
    mscr = af.alloc(512)

    def pair_weights(p, extra=()):
        bb = p % 2
        op("gpsimd", "dma_start", *chunked(wqb_b[bb], wqb[:, p * 384:(p + 1) * 384], 4),
           waits=list(extra), inc=sG.inc(16))
        op("gpsimd", "dma_start", *chunked(wkvb_b[bb], wkvb[:, p * 512:(p + 1) * 512], 4),
           inc=sG.inc(16))
        op("gpsimd", "dma_start", *chunked(wout_b[bb], wout[p * 256:(p + 1) * 256, :], 2),
           inc=sG.inc(16))
        return sG.at()

    wready = [None] * 8
    wready[0] = pair_weights(0, [phase0_pe])
    wready[1] = pair_weights(1, [phase0_pe])
    pair_pe_done = {-1: None}

    for p in range(8):
        bb = p % 2
        gen_w = [wready[p], qa_done, kpe_done]
        first_gen = True
        for hh in range(2):
            for n in range(2):
                bank = n
                for k in range(4):
                    w = []
                    if k == 0:
                        if first_gen:
                            w += gen_w
                            first_gen = False
                        if bfw(("g01", bank)):
                            w.append(bfw(("g01", bank)))
                    op("tensor", "matmul", psf(bank),
                       wqb_b[bb][:, k * 384 + hh * 192: k * 384 + hh * 192 + 128],
                       qa_b[:, k * 1024 + n * 512: k * 1024 + (n + 1) * 512],
                       start=(k == 0), stop=(k == 3), waits=w,
                       inc=(sP.inc() if k == 3 else None))
                op("scalar", "copy",
                   qn2[:, hh * 1024 + n * 512: hh * 1024 + (n + 1) * 512],
                   psf(bank), waits=[sP.at()], inc=sS.inc())
                bank_free[("g01", bank)] = sS.at()
            for n in range(2):
                bank = 2 + n
                for k in range(4):
                    w = []
                    if k == 0 and bfw(("g23", bank)):
                        w.append(bfw(("g23", bank)))
                    op("tensor", "matmul", psf(bank, 0, 512, 64),
                       wqb_b[bb][:, k * 384 + hh * 192 + 128: k * 384 + hh * 192 + 192],
                       qa_b[:, k * 1024 + n * 512: k * 1024 + (n + 1) * 512],
                       start=(k == 0), stop=(k == 3), waits=w,
                       inc=(sP.inc() if k == 3 else None))
            op("vector", "tensor_copy", qpe_f[:, 0:512], psf(2, 0, 512, 64), waits=[sP.at()])
            op("vector", "tensor_copy", qpe_f[:, 512:1024], psf(3, 0, 512, 64), inc=sV.inc())
            bank_free[("g23", 2)] = sV.at()
            bank_free[("g23", 3)] = sV.at()
            qpe_evac = sV.at()
            for n in range(2):
                bank = 4 + n
                w = [qpe_evac] if n == 0 else []
                if bfw(("g45", bank)):
                    w.append(bfw(("g45", bank)))
                if bfw(("ptb", bank)):
                    w.append(bfw(("ptb", bank)))
                op("tensor", "matmul", psf(bank, 0, 512, 64), SWPF[:],
                   qpe_f[:, n * 512:(n + 1) * 512],
                   start=True, stop=True, waits=w, inc=(sP.inc() if n == 1 else None))
            op("vector", "tensor_mul", qpe_f, qpe_f, csq_s[0:64, :], waits=[sP.at()])
            for n in range(2):
                op("vector", "tensor_mul", qtmp[:, n * 512:(n + 1) * 512],
                   psf(4 + n, 0, 512, 64),
                   csq_s[64:128, n * 512:(n + 1) * 512],
                   inc=(sV.inc() if n == 1 else None))
            for n in range(2):
                bank_free[("g45", 4 + n)] = sV.at()
            for n in range(2):
                op("vector", "tensor_add",
                   qp2[:, hh * 1024 + n * 512: hh * 1024 + (n + 1) * 512],
                   qpe_f[:, n * 512:(n + 1) * 512], qtmp[:, n * 512:(n + 1) * 512],
                   inc=(sV.inc() if n == 1 else None))
            for n in range(4):
                bank = n % 2
                for k in range(4):
                    w = []
                    if k == 0 and bfw(("g01", bank)):
                        w.append(bfw(("g01", bank)))
                    op("tensor", "matmul", psf(bank),
                       wkvb_b[bb][:, k * 512 + hh * 256: k * 512 + hh * 256 + 128],
                       comp_b[:, k * 2048 + n * 512: k * 2048 + (n + 1) * 512],
                       start=(k == 0), stop=(k == 3), waits=w,
                       inc=(sP.inc() if k == 3 else None))
                op("scalar", "copy",
                   kf2[:, hh * 2048 + n * 512: hh * 2048 + (n + 1) * 512],
                   psf(bank), waits=[sP.at()], inc=sS.inc())
                bank_free[("g01", bank)] = sS.at()
            for kt in range(16):
                bank = 2 + kt % 2
                for k in range(4):
                    w = []
                    if k == 0 and bfw(("g23", bank)):
                        w.append(bfw(("g23", bank)))
                    op("tensor", "matmul", psf(bank, 0, 128),
                       comp_b[:, k * 2048 + kt * 128: k * 2048 + (kt + 1) * 128],
                       wkvb_b[bb][:, k * 512 + hh * 256 + 128: k * 512 + hh * 256 + 256],
                       start=(k == 0), stop=(k == 3), waits=w,
                       inc=(sP.inc() if k == 3 else None))
                op("scalar", "copy",
                   v2[:, kt * 256 + hh * 128: kt * 256 + hh * 128 + 128],
                   psf(bank, 0, 128), waits=[sP.at()], inc=sS.inc())
                bank_free[("g23", bank)] = sS.at()
        gen_done_s = sS.at()
        gen_done_v = sV.at()

        units = [(hh, j) for hh in range(2) for j in range(8)]
        BIAS_, SC_, RS0_ = 6, 1, 7
        score_done = {}
        emitted = [0] * len(units)

        def emit_score(ui, c):
            hh, j = units[ui]
            w = [gen_done_s, gen_done_v] if (ui == 0 and c == 0) else []
            if bfw(("sc", c)):
                w.append(bfw(("sc", c)))
            op("tensor", "matmul", psf(c),
               qn2[:, hh * 1024 + j * 128: hh * 1024 + (j + 1) * 128],
               kf2[:, hh * 2048 + c * 512: hh * 2048 + (c + 1) * 512],
               start=True, stop=False, waits=w)
            op("tensor", "matmul", psf(c),
               qp2[:, hh * 1024 + j * 128: hh * 1024 + (j + 1) * 128],
               kpe_b[:, c * 512:(c + 1) * 512],
               start=False, stop=True, inc=sP.inc())
            score_done[(ui, c)] = sP.at()
            emitted[ui] = c + 1

        def ensure_scores(ui):
            if ui < len(units):
                for c in range(emitted[ui], KCJ[units[ui][1]]):
                    emit_score(ui, c)

        ensure_scores(0)
        for ui, (hh, j) in enumerate(units):
            kc = KCJ[j]
            slot = ui % 2
            avb = 6 + slot
            ensure_scores(ui)
            op("vector", "tensor_reduce", smv(T0_), PS[:, 0:kc * 512], AX.X, ALU.max,
               negate=True, waits=[score_done[(ui, kc - 1)]] + ([consts_g] if ui == 0 else []))
            op("vector", "tensor_scalar_mul", SM[:, BIAS_:BIAS_ + 1], smv(T0_),
               C_ATT, inc=sV.inc())
            vmile = sV.at()
            pexps = []
            for c in range(kc):
                sw = [vmile] if c == 0 else []
                if bfw(("psb", c)):
                    sw.append(bfw(("psb", c)))
                op("scalar", "activation", psb_[c], psf(c), AF.Exp,
                   bias=SM[:, BIAS_:BIAS_ + 1], scale=C_ATT,
                   accum_out=SM[:, RS0_ + c:RS0_ + c + 1], waits=sw, inc=sS.inc())
                pexp = sS.at()
                bank_free[("sc", c)] = pexp
                if c == kc - 1:
                    op("vector", "tensor_mul", psb_[c], psb_[c],
                       MQ[:, j * 512:(j + 1) * 512], waits=[pexp], inc=sV.inc())
                    op("scalar", "activation", psb_[c], psb_[c], AF.Copy,
                       accum_out=SM[:, RS0_ + c:RS0_ + c + 1],
                       waits=[sV.at()], inc=sS.inc())
                    pexp = sS.at()
                pexps.append(pexp)
                if ui + 1 < len(units) and c < KCJ[units[ui + 1][1]]:
                    emit_score(ui + 1, c)
            ptcpys = []
            for c in range(kc):
                tb = 4 + c % 2
                for i in range(4):
                    w = [pexps[c]] if i == 0 else []
                    if i == 0 and bfw(("ptb", tb)):
                        w.append(bfw(("ptb", tb)))
                    op("tensor", "transpose", psb(tb, i * 64, 128),
                       psb_[c][:, i * 128:(i + 1) * 128], IDB[:],
                       waits=w, inc=(sP.inc() if i == 3 else None))
                bank_free[("psb", c)] = sP.at()
                op("vector", "tensor_copy", ptb_[c], psb(tb, 0, 512), waits=[sP.at()],
                   inc=sV.inc())
                ptcpys.append(sV.at())
                bank_free[("ptb", tb)] = sV.at()
            for c in range(kc):
                for i in range(4):
                    w = [ptcpys[c]] if i == 0 else []
                    if i == 0 and c == 0 and bfw(("av", avb)):
                        w.append(bfw(("av", avb)))
                    op("tensor", "matmul", psf(avb, 0, 128),
                       ptb_[c][:, i * 128:(i + 1) * 128],
                       v2[:, (c * 4 + i) * 256 + hh * 128: (c * 4 + i) * 256 + hh * 128 + 128],
                       start=(c == 0 and i == 0), stop=(c == kc - 1 and i == 3), waits=w,
                       inc=(sP.inc() if (c == kc - 1 and i == 3) else None))
            avdone = sP.at()
            ensure_scores(ui + 1)
            op("scalar", "copy", SM[:, SC_:SC_ + 1], SM[:, RS0_:RS0_ + 1])
            for c in range(1, kc):
                op("scalar", "add", SM[:, SC_:SC_ + 1], SM[:, SC_:SC_ + 1],
                   SM[:, RS0_ + c:RS0_ + c + 1])
            op("scalar", "copy", SM[:, SC_ + 1:SC_ + 2], SM[:, SC_:SC_ + 1],
               inc=sS.inc())
            schain = sS.at()
            op("vector", "reciprocal", SM[:, 12:14], SM[:, SC_:SC_ + 2],
               waits=[schain], inc=sV.inc())
            vrec = sV.at()
            op("scalar", "mul", OTOK[:], psf(avb, 0, 128), SM[:, 12:13],
               waits=[vrec, avdone], inc=sS.inc())
            ow = sS.at()
            op("tensor", "transpose", psb(avb, 128, 128), OTOK[:], IDB[:], waits=[ow],
               inc=sP.inc())
            op("scalar", "copy",
               ofm[:, hh * 1024 + j * 128: hh * 1024 + (j + 1) * 128],
               psb(avb, 128, 128), waits=[sP.at()], inc=sS.inc())
            bank_free[("av", avb)] = sS.at()
        attn_units_done = sS.at()

        for m in range(16):
            for n in range(2):
                bank = (m * 2 + n) % 2
                for ch in range(2):
                    w = []
                    if ch == 0:
                        w = [attn_units_done] if (m == 0 and n == 0) else []
                        if bfw(("pr", bank)):
                            w.append(bfw(("pr", bank)))
                    op("tensor", "matmul", psf(bank),
                       wout_b[bb][:, ch * 2048 + m * 128: ch * 2048 + (m + 1) * 128],
                       ofm[:, ch * 1024 + n * 512: ch * 1024 + (n + 1) * 512],
                       start=(ch == 0), stop=(ch == 1), waits=w,
                       inc=(sP.inc() if ch == 1 else None))
                op("vector", "tensor_add",
                   XFM[:, m * 1024 + n * 512: m * 1024 + (n + 1) * 512],
                   XFM[:, m * 1024 + n * 512: m * 1024 + (n + 1) * 512], psf(bank),
                   waits=[sP.at()], inc=sV.inc())
                bank_free[("pr", bank)] = sV.at()
        bank_free[("g01", 0)] = bank_free[("pr", 0)]
        bank_free[("g01", 1)] = bank_free[("pr", 1)]
        pair_pe_done[p] = sP.at()
        if p + 2 < 8:
            wready[p + 2] = pair_weights(p + 2, [pair_pe_done[p]])
    attn_done_v = sV.at()
    attn_done_s = sS.at()
    attn_pe = sP.at()

    if phases <= 3:
        return _finish_debug()

    ab.reset(0)
    x1b = ab.alloc(16 * 1024)
    sqb = ab.alloc(16 * 1024)
    af.reset(0)
    LNR = af.alloc(3072, parts=1)
    for k in range(16):
        op("scalar", "activation", sqb[:, k * 1024:(k + 1) * 1024],
           XFM[:, k * 1024:(k + 1) * 1024], AF.Square,
           waits=([attn_done_v, attn_done_s] if k == 0 else []),
           inc=(sS.inc() if k == 15 else None))
    cast_s = sS.at()
    for n in range(2):
        for k in range(16):
            w = [attn_done_v, cast_s, ones_q, ones_g] if (n == 0 and k == 0) else []
            op("tensor", "matmul", psf(n, 0, 512, 1), ONEFC[:],
               XFM[:, k * 1024 + n * 512: k * 1024 + (n + 1) * 512],
               start=(k == 0), stop=(k == 15), waits=w)
        for k in range(16):
            op("tensor", "matmul", psf(2 + n, 0, 512, 1), ONEB[:],
               sqb[:, k * 1024 + n * 512: k * 1024 + (n + 1) * 512],
               start=(k == 0), stop=(k == 15),
               inc=(sP.inc() if (n == 1 and k == 15) else None))
    sums_pe = sP.at()
    op("scalar", "copy", LNR[:, 0:512], psf(0, 0, 512, 1), waits=[sums_pe])
    op("scalar", "copy", LNR[:, 512:1024], psf(1, 0, 512, 1))
    op("scalar", "copy", LNR[:, 1024:1536], psf(2, 0, 512, 1))
    op("scalar", "copy", LNR[:, 1536:2048], psf(3, 0, 512, 1), inc=sS.inc())
    rows_s = sS.at()
    op("vector", "tensor_scalar", LNR[:, 1024:2048], LNR[:, 1024:2048], 1.0 / D, EPS,
       ALU.mult, ALU.add, waits=[rows_s])
    op("vector", "tensor_scalar_mul", LNR[:, 0:1024], LNR[:, 0:1024], 1.0 / D,
       inc=sV.inc())
    mean_v = sV.at()
    op("scalar", "activation", LNR[:, 2048:3072], LNR[:, 0:1024], AF.Square,
       waits=[mean_v], inc=sS.inc())
    m2_s = sS.at()
    op("vector", "tensor_sub", LNR[:, 1024:2048], LNR[:, 1024:2048], LNR[:, 2048:3072],
       waits=[m2_s], inc=sV.inc())
    stats_v = sV.at()
    op("scalar", "activation", LNR[:, 1024:2048], LNR[:, 1024:2048], AF.Relu,
       waits=[stats_v])
    op("scalar", "activation", LNR[:, 1024:2048], LNR[:, 1024:2048], AF.Ln,
       bias=EPSC[0:1, 0:1])
    op("scalar", "activation", LNR[:, 1024:2048], LNR[:, 1024:2048], AF.Exp, scale=-0.5,
       inc=sS.inc())
    rstd_s = sS.at()
    op("vector", "tensor_mul", LNR[:, 0:1024], LNR[:, 0:1024], LNR[:, 1024:2048],
       waits=[rstd_s], inc=sV.inc())
    meanr_v = sV.at()
    for n in range(2):
        op("tensor", "matmul", psf(n), ONEF[:],
           LNR[0:1, 1024 + n * 512:1024 + (n + 1) * 512],
           start=True, stop=True, waits=([meanr_v] if n == 0 else []))
        op("tensor", "matmul", psf(2 + n), ONEF[:], LNR[0:1, n * 512:(n + 1) * 512],
           start=True, stop=True, inc=(sP.inc() if n == 1 else None))
    bcast_pe = sP.at()
    for k in range(16):
        for n in range(2):
            sl = slice(k * 1024 + n * 512, k * 1024 + (n + 1) * 512)
            w = [bcast_pe] if (k == 0 and n == 0) else []
            op("vector", "tensor_mul", XFM[:, sl], XFM[:, sl], psf(n), waits=w)
            op("vector", "tensor_sub", XFM[:, sl], XFM[:, sl], psf(2 + n))
            op("vector", "tensor_copy", x1b[:, sl], XFM[:, sl],
               inc=(sV.inc() if (k == 15 and n == 1) else None))
    ln1_done = sV.at()

    if phases <= 4:
        return _finish_debug()

    ab.reset(16 * 1024)
    af.reset(5120)
    rg_f = af.alloc(2048, parts=8)
    rT_f = af.alloc(128)
    af.reset(0)
    lg_f = af.alloc(1024, parts=8)
    GFM = af.alloc(1024, parts=8)
    af_router = af.mark()
    ltok = af.alloc(64)
    gtok = af.alloc(64)
    sigt = af.alloc(64)
    op("sync", "dma_start", rg_f, rgate, waits=[attn_pe], inc=sQ.inc(16))
    rg_g = sQ.at()
    for kg in range(2):
        bank = 4 + kg
        for ki in range(8):
            k = kg * 8 + ki
            w = [rg_g, bfw(("ptb", 4)), bfw(("ptb", 5))] if k == 0 else []
            op("tensor", "transpose", psf(bank, ki * 8, 8), rg_f[:, k * 128:(k + 1) * 128],
               IDF[0:8, 0:8], waits=w, inc=(sP.inc() if ki == 7 else None))
        op("scalar", "copy", rT_f[:, kg * 64:(kg + 1) * 64], psf(bank, 0, 64),
           waits=[sP.at()], inc=sS.inc())
    rT_done = sS.at()
    for n in range(2):
        for k in range(16):
            w = [rT_done, ln1_done] if (n == 0 and k == 0) else []
            op("tensor", "matmul", psf(n, 0, 512, 8), rT_f[:, k * 8:(k + 1) * 8],
               XFM[:, k * 1024 + n * 512: k * 1024 + (n + 1) * 512],
               start=(k == 0), stop=(k == 15), waits=w,
               inc=(sP.inc() if k == 15 else None))
    lg_pe = sP.at()
    op("vector", "tensor_copy", lg_f[:, 0:512], psf(0, 0, 512, 8), waits=[lg_pe])
    op("vector", "tensor_copy", lg_f[:, 512:1024], psf(1, 0, 512, 8), inc=sV.inc())
    lg_v = sV.at()
    for tt in range(8):
        w = [lg_v] if tt == 0 else []
        op("tensor", "transpose", psf(2, tt * 8, 8), lg_f[:, tt * 128:(tt + 1) * 128],
           IDF[0:8, 0:8], waits=w, inc=(sP.inc() if tt == 7 else None))
    op("vector", "tensor_copy", ltok, psf(2, 0, 64), waits=[sP.at()], inc=sV.inc())
    V1C, V2C = 16, 24
    for tt in range(8):
        op("vector", "reduce_max", SM[:, V1C + tt:V1C + tt + 1],
           ltok[:, tt * 8:(tt + 1) * 8], AX.X)
    for tt in range(8):
        op("vector", "tensor_scalar", gtok[:, tt * 8:(tt + 1) * 8],
           ltok[:, tt * 8:(tt + 1) * 8], SM[:, V1C + tt:V1C + tt + 1], NEG,
           ALU.is_equal, ALU.mult)
    for tt in range(8):
        op("vector", "tensor_add", gsc, ltok[:, tt * 8:(tt + 1) * 8],
           gtok[:, tt * 8:(tt + 1) * 8]) if False else None
    for tt in range(8):
        op("vector", "tensor_add", gtok[:, tt * 8:(tt + 1) * 8],
           gtok[:, tt * 8:(tt + 1) * 8], ltok[:, tt * 8:(tt + 1) * 8])
    for tt in range(8):
        op("vector", "reduce_max", SM[:, V2C + tt:V2C + tt + 1],
           gtok[:, tt * 8:(tt + 1) * 8], AX.X,
           inc=(sV.inc() if tt == 7 else None))
    vtop = sV.at()
    for tt in range(8):
        op("scalar", "add", SM[:, V1C + tt:V1C + tt + 1], SM[:, V1C + tt:V1C + tt + 1],
           SM[:, V2C + tt:V2C + tt + 1], waits=([vtop] if tt == 0 else []))
        op("scalar", "mul", SM[:, V1C + tt:V1C + tt + 1], SM[:, V1C + tt:V1C + tt + 1],
           -1.0, inc=(sS.inc() if tt == 7 else None))
    s12done = sS.at()
    for tt in range(8):
        op("vector", "tensor_scalar", gtok[:, tt * 8:(tt + 1) * 8],
           ltok[:, tt * 8:(tt + 1) * 8], 2.0, SM[:, V1C + tt:V1C + tt + 1],
           ALU.mult, ALU.add, waits=([s12done] if tt == 0 else []),
           inc=(sV.inc() if tt == 7 else None))
    zdone = sV.at()
    op("scalar", "activation", sigt, gtok[:, 0:64], AF.Sigmoid, waits=[zdone],
       inc=sS.inc())
    sigdone = sS.at()
    for tt in range(8):
        op("vector", "tensor_scalar", gtok[:, tt * 8:(tt + 1) * 8],
           ltok[:, tt * 8:(tt + 1) * 8], SM[:, V2C + tt:V2C + tt + 1], None,
           ALU.is_ge, waits=([sigdone] if tt == 0 else []))
    for tt in range(8):
        op("vector", "tensor_mul", gtok[:, tt * 8:(tt + 1) * 8],
           gtok[:, tt * 8:(tt + 1) * 8], sigt[:, tt * 8:(tt + 1) * 8],
           inc=(sV.inc() if tt == 7 else None))
    gtok_v = sV.at()

    for tt in range(8):
        bank = 3 + tt // 4
        dst = PS[0:8, bank * 512 + (tt % 4) * 128: bank * 512 + (tt % 4) * 128 + 128]
        w = [gtok_v] if tt == 0 else []
        op("tensor", "transpose", dst, gtok[:, tt * 8:(tt + 1) * 8], IDF[:],
           waits=w, inc=(sP.inc() if tt == 7 else None))
    op("vector", "tensor_copy", GFM, PS[0:8, 3 * 512: 3 * 512 + 1024],
       waits=[sP.at()], inc=sV.inc())
    router_done = sV.at()
    router_pe = sP.at()

    if phases <= 5:
        return _finish_debug()

    ab.reset(16 * 1024)
    wgu_b = [ab.alloc(16 * 256) for _ in range(2)]
    wdn_b = ab.alloc(8 * 2048)
    h_b = ab.alloc(8 * 1024)
    af.reset(af_router)
    GROW = af.alloc(1024, parts=1)
    gbc = af.alloc(1024)
    silu_s = af.alloc(1024)

    def moe_wgu(e, q8, extra=()):
        op("gpsimd", "dma_start", *chunked(wgu_b[q8 % 2], wgu[e, :, q8 * 256:(q8 + 1) * 256], 16),
           waits=list(extra), inc=sG.inc(16))
        return sG.at()

    wg_ready = {(0, 0): moe_wgu(0, 0, [router_done, router_pe]),
                (0, 1): moe_wgu(0, 1)}
    moe_pe_after = router_pe
    gu_pe_prev = router_pe

    for e in range(NE):
        wd_ready = None
        op("sync", "dma_start", GROW, GFM[e:e + 1, :],
           waits=[router_done] + ([bank_free.get(("grow",))] if bank_free.get(("grow",)) else []),
           inc=sQ.inc(16))
        grow_q = sQ.at()
        for n in range(2):
            bank = 6 + n
            w = [grow_q] if n == 0 else []
            if bfw(("gb", bank)):
                w.append(bfw(("gb", bank)))
            op("tensor", "matmul", psf(bank), ONEF[:], GROW[:, n * 512:(n + 1) * 512],
               start=True, stop=True, waits=w, inc=(sP.inc() if n == 1 else None))
        bank_free[("grow",)] = sP.at()
        op("vector", "tensor_copy", gbc[:, 0:512], psf(6), waits=[sP.at()])
        op("vector", "tensor_copy", gbc[:, 512:1024], psf(7), inc=sV.inc())
        bank_free[("gb", 6)] = sV.at()
        bank_free[("gb", 7)] = sV.at()
        gb_done = sV.at()
        for q8 in range(8):
            nxt = (e, q8 + 1) if q8 + 1 < 8 else (e + 1, 0)
            if nxt[0] < NE and nxt not in wg_ready:
                wg_ready[nxt] = moe_wgu(nxt[0], nxt[1], [gu_pe_prev])
            if q8 == 2:
                op("gpsimd", "dma_start", *chunked(wdn_b, wdn[e], 8),
                   waits=[moe_pe_after], inc=sG.inc(16))
                wd_ready = sG.at()
            wrd = wg_ready.pop((e, q8))
            for n in range(2):
                for k in range(16):
                    w = []
                    if k == 0:
                        w = [wrd]
                        if e == 0 and q8 == 0 and n == 0:
                            w.append(ln1_done)
                        if bfw(("gu", n)):
                            w.append(bfw(("gu", n)))
                        if bfw(("gu", 2 + n)):
                            w.append(bfw(("gu", 2 + n)))
                    op("tensor", "matmul", psf(n),
                       wgu_b[q8 % 2][:, k * 256: k * 256 + 256: 2],
                       x1b[:, k * 1024 + n * 512: k * 1024 + (n + 1) * 512],
                       start=(k == 0), stop=(k == 15), waits=w)
                    op("tensor", "matmul", psf(2 + n),
                       wgu_b[q8 % 2][:, k * 256 + 1: k * 256 + 256: 2],
                       x1b[:, k * 1024 + n * 512: k * 1024 + (n + 1) * 512],
                       start=(k == 0), stop=(k == 15),
                       inc=(sP.inc() if (n == 1 and k == 15) else None))
            gu_pe = sP.at()
            gu_pe_prev = gu_pe
            for n in range(2):
                sl = slice(n * 512, (n + 1) * 512)
                w = [gu_pe, gb_done] if n == 0 else []
                op("scalar", "activation", silu_s[:, sl], psf(n), AF.Silu, waits=w,
                   inc=sS.inc())
                op("vector", "tensor_mul", silu_s[:, sl], silu_s[:, sl], psf(2 + n),
                   waits=[sS.at()])
                op("vector", "tensor_mul",
                   h_b[:, q8 * 1024 + n * 512: q8 * 1024 + (n + 1) * 512],
                   silu_s[:, sl], gbc[:, sl], inc=sV.inc())
                bank_free[("gu", n)] = sV.at()
                bank_free[("gu", 2 + n)] = sV.at()
        gu_done = sV.at()
        for m in range(16):
            for n in range(2):
                bank = 4 + (m * 2 + n) % 2
                for i in range(8):
                    w = []
                    if i == 0:
                        w = [wd_ready, gu_done] if (m == 0 and n == 0) else []
                        if bfw(("dn", bank)):
                            w.append(bfw(("dn", bank)))
                    op("tensor", "matmul", psf(bank),
                       wdn_b[:, i * 2048 + m * 128: i * 2048 + (m + 1) * 128],
                       h_b[:, i * 1024 + n * 512: i * 1024 + (n + 1) * 512],
                       start=(i == 0), stop=(i == 7), waits=w,
                       inc=(sP.inc() if i == 7 else None))
                op("vector", "tensor_add",
                   XFM[:, m * 1024 + n * 512: m * 1024 + (n + 1) * 512],
                   XFM[:, m * 1024 + n * 512: m * 1024 + (n + 1) * 512], psf(bank),
                   waits=[sP.at()], inc=sV.inc())
                bank_free[("dn", bank)] = sV.at()
        moe_pe_after = sP.at()
    moe_done_v = sV.at()

    if phases <= 6:
        return _finish_debug()

    af.reset(0)
    ost = [af.alloc(2048) for _ in range(2)]
    osq = af.alloc(2048)
    for j in range(8):
        bb = j % 2
        for cg in range(4):
            bank = cg % 2
            for ci in range(4):
                c = cg * 4 + ci
                pv = psf(bank, ci * 128, 128)
                w = []
                if ci == 0:
                    if j == 0 and cg == 0:
                        w.append(moe_done_v)
                    if bfw(("o", bank)):
                        w.append(bfw(("o", bank)))
                op("tensor", "transpose", pv,
                   XFM[:, c * 1024 + j * 128: c * 1024 + (j + 1) * 128],
                   IDF[:], waits=w, inc=(sP.inc() if ci == 3 else None))
            cw = [sP.at()]
            if cg == 0 and j >= 2:
                cw.append((sO.sem, (j - 1) * 16))
            op("scalar", "copy", ost[bb][:, cg * 512:(cg + 1) * 512], psf(bank),
               waits=cw, inc=sS.inc())
            bank_free[("o", bank)] = sS.at()
        tile_s = sS.at()
        op("vector", "reduce_sum", smv(T0_), ost[bb], AX.X, waits=[tile_s],
           inc=sV.inc())
        sq_w = [tile_s]
        if j > 0:
            sq_w.append(bank_free.get(("osq",)))
        op("scalar", "activation", osq, ost[bb], AF.Square, waits=sq_w, inc=sS.inc())
        op("vector", "reduce_sum", smv(T1_), osq, AX.X, waits=[sS.at()], inc=sV.inc())
        bank_free[("osq",)] = sV.at()
        op("scalar", "mul", smv(T0_), smv(T0_), 1.0 / D, waits=[sV.at()])
        op("scalar", "activation", smv(T2_), smv(T0_), AF.Square)
        op("scalar", "mul", smv(T2_), smv(T2_), -1.0)
        op("scalar", "mul", smv(T1_), smv(T1_), 1.0 / D)
        op("scalar", "add", smv(T1_), smv(T1_), smv(T2_))
        op("scalar", "activation", smv(T1_), smv(T1_), AF.Relu)
        op("scalar", "activation", smv(T1_), smv(T1_), AF.Ln, bias=EPSC[:, 0:1])
        op("scalar", "activation", smv(T1_), smv(T1_), AF.Exp, scale=-0.5, inc=sS.inc())
        op("vector", "tensor_scalar", ost[bb], ost[bb], smv(T0_), smv(T1_),
           ALU.subtract, ALU.mult, waits=[sS.at()], inc=sV.inc())
        op("sync", "dma_start", out[j * 128:(j + 1) * 128, :], ost[bb],
           waits=[sV.at()], inc=sO.inc(16))
    waitonly("sync", sO.at())

    _emit_all()
    es.close()
    return nc



def _rope_tables(positions, n):
    r = np.arange(ROPE // 2, dtype=np.float64)
    theta = 10000.0 ** (-2.0 * r / ROPE)
    ang = positions[None, :].astype(np.float64) * theta[:, None]
    C = np.empty((64, n), dtype=np.float64)
    S = np.empty((64, n), dtype=np.float64)
    C[0::2] = np.cos(ang)
    C[1::2] = np.cos(ang)
    S[0::2] = -np.sin(ang)
    S[1::2] = np.sin(ang)
    return np.concatenate([C, S], axis=0).astype(np.float32)


def kernel(L0_attn_x, L0_attn_Wqa, L0_attn_Wqb, L0_attn_Wkva, L0_attn_Wkvb,
           L0_attn_Wout, L0_ffn_router_gate, L0_ffn_expert_gate_up,
           L0_ffn_expert_down):
    x = np.ascontiguousarray(np.asarray(L0_attn_x, dtype=np.float32))
    wqa = np.ascontiguousarray(np.asarray(L0_attn_Wqa, dtype=np.float32))
    wqb = np.ascontiguousarray(np.asarray(L0_attn_Wqb, dtype=np.float32))
    wkva = np.ascontiguousarray(np.asarray(L0_attn_Wkva, dtype=np.float32))
    wkvb = np.ascontiguousarray(np.asarray(L0_attn_Wkvb, dtype=np.float32))
    wout = np.ascontiguousarray(np.asarray(L0_attn_Wout, dtype=np.float32))
    rg = np.ascontiguousarray(np.asarray(L0_ffn_router_gate, dtype=np.float32))
    wgu = np.ascontiguousarray(np.asarray(L0_ffn_expert_gate_up, dtype=np.float32))
    wdn = np.ascontiguousarray(np.asarray(L0_ffn_expert_down, dtype=np.float32))

    ident = np.eye(128, dtype=np.float32)
    swp = np.zeros((64, 64), dtype=np.float32)
    for i in range(32):
        swp[2 * i, 2 * i + 1] = 1.0
        swp[2 * i + 1, 2 * i] = 1.0
    csk = _rope_tables(np.arange(NK), NK)

    in_maps = []
    for core in range(8):
        b, h = core // 2, core % 2
        qs = QSETS[h]
        rows = np.concatenate([np.arange(q * 128, (q + 1) * 128) for q in qs])
        xq_h = np.ascontiguousarray(x[b][rows])
        mask = np.empty((128, 8 * 512), dtype=np.float32)
        for j, q in enumerate(qs):
            kc = KCJ[j]
            rr = q * 128 + np.arange(128)[:, None]
            cc = (kc - 1) * 512 + np.arange(512)[None, :]
            mask[:, j * 512:(j + 1) * 512] = np.where(cc <= rr, 1.0, 0.0)
        csq_h = _rope_tables(rows, NQ)
        in_maps.append({
            "xq": xq_h, "xkv": x[b], "wqa": wqa, "wqb": wqb, "wkva": wkva,
            "wkvb": wkvb, "wout": wout, "rgate": rg, "wgu": wgu, "wdn": wdn,
            "maskin": mask, "csq": csq_h, "csk": csk, "identin": ident,
            "swpin": swp, "onesrow": np.ones((1, 128), np.float32),
            "onescol": np.ones((128, 1), np.float32),
            "epscol": np.full((128, 1), 1e-5, np.float32),
        })

    nc = build_program(PHASES)
    res = run_bass_kernel_spmd(nc, in_maps, core_ids=list(range(8)))

    outp = np.empty((4, 2048, 2048), dtype=np.float32)
    for core in range(8):
        b, h = core // 2, core % 2
        o = np.asarray(res.results[core]["out"], dtype=np.float32)
        for j, q in enumerate(QSETS[h]):
            outp[b, q * 128:(q + 1) * 128, :] = o[j * 128:(j + 1) * 128, :]
    return outp


# version 14
# speedup vs baseline: 1.0248x; 1.0248x over previous
import math
import contextlib
import numpy as np
import concourse.bass as bass
import concourse.mybir as mybir
from concourse.bass_utils import run_bass_kernel_spmd

F32 = mybir.dt.float32
BF16 = mybir.dt.bfloat16
AF = mybir.ActivationFunctionType
ALU = mybir.AluOpType
AX = mybir.AxisListType

D = 2048
HEADS = 16
NOPE = 128
ROPE = 64
VD = 128
QH = NOPE + ROPE
KVR = 512
NQ = 1024
NK = 2048
NE = 8
INTER = 1024
C_ATT = 1.0 / math.sqrt(QH)
EPS = 1e-5
KCJ = [1, 1, 2, 2, 3, 3, 4, 4]
QSETS = [[0, 3, 4, 7, 8, 11, 12, 15], [1, 2, 5, 6, 9, 10, 13, 14]]
NEG = -1.0e30


class Ctr:
    def __init__(self, sem):
        self.sem = sem
        self.n = 0

    def inc(self, k=1):
        self.n += k
        return (self.sem, k)

    def at(self):
        return (self.sem, self.n)


PHASES = 7


def build_program(phases=None):
    if phases is None:
        phases = PHASES
    nc = bass.Bass("TRN2", target_bir_lowering=False)

    xq = nc.dram_tensor("xq", [NQ, D], F32, kind="ExternalInput").ap()
    xkv = nc.dram_tensor("xkv", [NK, D], F32, kind="ExternalInput").ap()
    wqa = nc.dram_tensor("wqa", [D, KVR], F32, kind="ExternalInput").ap()
    wqb = nc.dram_tensor("wqb", [KVR, HEADS * QH], F32, kind="ExternalInput").ap()
    wkva = nc.dram_tensor("wkva", [D, KVR + ROPE], F32, kind="ExternalInput").ap()
    wkvb = nc.dram_tensor("wkvb", [KVR, HEADS * (NOPE + VD)], F32, kind="ExternalInput").ap()
    wout = nc.dram_tensor("wout", [HEADS * VD, D], F32, kind="ExternalInput").ap()
    rgate = nc.dram_tensor("rgate", [NE, D], F32, kind="ExternalInput").ap()
    wgu = nc.dram_tensor("wgu", [NE, D, 2 * INTER], F32, kind="ExternalInput").ap()
    wdn = nc.dram_tensor("wdn", [NE, INTER, D], F32, kind="ExternalInput").ap()
    maskin = nc.dram_tensor("maskin", [128, 8 * 512], F32, kind="ExternalInput").ap()
    csq = nc.dram_tensor("csq", [128, NQ], F32, kind="ExternalInput").ap()
    csk = nc.dram_tensor("csk", [128, NK], F32, kind="ExternalInput").ap()
    identin = nc.dram_tensor("identin", [128, 128], F32, kind="ExternalInput").ap()
    swpin = nc.dram_tensor("swpin", [64, 64], F32, kind="ExternalInput").ap()
    onesrow = nc.dram_tensor("onesrow", [1, 128], F32, kind="ExternalInput").ap()
    onescol = nc.dram_tensor("onescol", [128, 1], F32, kind="ExternalInput").ap()
    epscol = nc.dram_tensor("epscol", [128, 1], F32, kind="ExternalInput").ap()
    out = nc.dram_tensor("out", [NQ, D], F32, kind="ExternalOutput").ap()

    es = contextlib.ExitStack()

    def sbuf(name, shape, dt):
        return es.enter_context(nc.sbuf_tensor(name, shape, dt))

    XFM = sbuf("XFM", [128, 16 * 1024], F32)
    MQ = sbuf("MQ", [128, 8 * 512], BF16)
    IDF = sbuf("IDF", [128, 128], F32)
    IDB = sbuf("IDB", [128, 128], BF16)
    SWPF = sbuf("SWPF", [64, 64], F32)
    ONEB = sbuf("ONEB", [128, 1], BF16)
    ONEFC = sbuf("ONEFC", [128, 1], F32)
    ONEF = sbuf("ONEF", [1, 128], F32)
    SM = sbuf("SM", [128, 32], F32)
    OTOK = sbuf("OTOK", [128, 128], BF16)
    EPSC = sbuf("EPSC", [128, 1], F32)
    NM_, S_, AL_, DD_, T0_, T1_, T2_, T3_ = 0, 1, 2, 3, 4, 5, 6, 7

    def smv(c):
        return SM[:, c:c + 1]

    ABW = 49152
    AFW = 8448
    AB = sbuf("AB", [128, ABW], BF16)
    AFA = sbuf("AFA", [128, AFW], F32)

    class Arena:
        def __init__(self, t, width):
            self.t, self.w, self.off = t, width, 0

        def alloc(self, cols, parts=128):
            assert self.off + cols <= self.w, f"arena overflow {self.off}+{cols}>{self.w}"
            v = self.t[0:parts, self.off:self.off + cols]
            self.off += cols
            return v

        def mark(self):
            return self.off

        def reset(self, off=0):
            self.off = off

    ab = Arena(AB, ABW)
    af = Arena(AFA, AFW)

    qa_b = ab.alloc(4 * 1024)
    comp_b = ab.alloc(4 * 2048)
    kpe_b = ab.alloc(2048, parts=64)
    AB_P01 = ab.mark()
    csq_s = af.alloc(1024)
    AF_P01 = af.mark()

    PS = es.enter_context(nc.psum_tensor("PS", [128, 4096], F32))

    def psf(bank, off=0, w=512, parts=128):
        return PS[0:parts, bank * 512 + off:bank * 512 + off + w]

    def psb(bank, off_w=0, w_bf=512, parts=128):
        return PS[0:parts, bank * 512 + off_w:bank * 512 + off_w + w_bf // 2].bitcast(BF16)

    sQ = Ctr(es.enter_context(nc.semaphore("sQ")))
    sG = Ctr(es.enter_context(nc.semaphore("sG")))
    sP = Ctr(es.enter_context(nc.semaphore("sP")))
    sV = Ctr(es.enter_context(nc.semaphore("sV")))
    sS = Ctr(es.enter_context(nc.semaphore("sS")))
    sO = Ctr(es.enter_context(nc.semaphore("sO")))

    prog = {k: [] for k in ("sync", "gpsimd", "tensor", "vector", "scalar")}

    def op(eng, name, *args, waits=(), inc=None, **kw):
        waits = tuple(w for w in waits if w is not None)

        def f(e, name=name, args=args, kw=kw, waits=waits, inc=inc):
            for (sem, val) in waits:
                if val > 0:
                    e.wait_ge(sem, val)
            inst = getattr(e, name)(*args, **kw)
            if inc is not None:
                inst.then_inc(*inc)
            return inst
        prog[eng].append(f)
        if name == "dma_start" and inc is not None and inc[0] in (sQ.sem, sG.sem, sO.sem):
            ctr = sQ if inc[0] == sQ.sem else (sG if inc[0] == sG.sem else sO)
            waitonly(eng, ctr.at())

    def waitonly(eng, *waits):
        waits = tuple(waits)

        def f(e, waits=waits):
            for (sem, val) in waits:
                e.wait_ge(sem, val)
        prog[eng].append(f)

    def _emit_all():
        with nc.Block() as block:
            @block.sync
            def _(e):
                for f in prog["sync"]:
                    f(e)

            @block.gpsimd
            def _(e):
                for f in prog["gpsimd"]:
                    f(e)

            @block.tensor
            def _(e):
                for f in prog["tensor"]:
                    f(e)

            @block.vector
            def _(e):
                for f in prog["vector"]:
                    f(e)

            @block.scalar
            def _(e):
                for f in prog["scalar"]:
                    f(e)

    bank_free = {}

    def _finish_debug():
        dbgab = nc.dram_tensor("dbgab", [128, ABW], BF16, kind="ExternalOutput").ap()
        dbgaf = nc.dram_tensor("dbgaf", [128, AFW], F32, kind="ExternalOutput").ap()
        for j in range(8):
            op("sync", "dma_start", out[j * 128:(j + 1) * 128, :],
               XFM[:, j * 2048:(j + 1) * 2048],
               waits=([sP.at(), sV.at(), sS.at(), sG.at(), sQ.at()] if j == 0 else []),
               inc=sO.inc(16))
        dbgsm = nc.dram_tensor("dbgsm", [128, 32], F32, kind="ExternalOutput").ap()
        dbgot = nc.dram_tensor("dbgot", [128, 128], BF16, kind="ExternalOutput").ap()
        op("sync", "dma_start", dbgot, OTOK[:], inc=sO.inc(16))
        op("sync", "dma_start", dbgab, AB[:], inc=sO.inc(16))
        op("sync", "dma_start", dbgaf, AFA[:], inc=sO.inc(16))
        op("sync", "dma_start", dbgsm, SM[:], inc=sO.inc(16))
        waitonly("sync", sO.at())
        _emit_all()
        es.close()
        return nc


    def bfw(key):
        return bank_free.get(key)

    def chunked(dst2d, src2d, nk):
        c = src2d.shape[-1]
        return (dst2d.rearrange("p (k c) -> p k c", k=nk),
                src2d.rearrange("(k p) c -> p k c", p=128))

    op("sync", "dma_start", IDF[:], identin, inc=sQ.inc(16))
    op("sync", "dma_start", SWPF[:], swpin, inc=sQ.inc(16))
    op("sync", "dma_start", csq_s, csq, inc=sQ.inc(16))
    consts_q = sQ.at()
    op("gpsimd", "dma_start", MQ[:], maskin, inc=sG.inc(16))
    op("gpsimd", "dma_start", IDB[:], identin, inc=sG.inc(16))
    consts_g = sG.at()
    op("sync", "dma_start", ONEF[:], onesrow, inc=sQ.inc(16))
    op("sync", "dma_start", ONEFC[:], onescol, inc=sQ.inc(16))
    op("gpsimd", "dma_start", ONEB[:], onescol, inc=sG.inc(16))
    op("sync", "dma_start", EPSC[:], epscol, inc=sQ.inc(16))
    ones_q = sQ.at()
    ones_g = sG.at()

    if phases <= 0.5:
        return _finish_debug()

    p0_ab = ab.mark()
    p0_af = af.mark()
    xqb = ab.alloc(16 * 1024)
    wqa_b = ab.alloc(16 * 512)
    stagf = [af.alloc(2048) for _ in range(2)]

    op("gpsimd", "dma_start", *chunked(wqa_b, wqa, 16), inc=sG.inc(16))
    wqa_g = sG.at()

    tile_tp_done = {}
    for j in range(8):
        w = [] if j < 2 else [tile_tp_done[("xq", j - 2)]]
        op("sync", "dma_start", stagf[j % 2], xq[j * 128:(j + 1) * 128, :],
           waits=w, inc=sQ.inc(16))
        tload = sQ.at()
        XFM3 = XFM.rearrange("p (k t) -> p k t", t=1024)
        for cg in range(4):
            bank = cg % 2
            for ci in range(4):
                c = cg * 4 + ci
                pv = psf(bank, ci * 128, 128)
                w = []
                if ci == 0:
                    if cg == 0:
                        w = [tload, consts_q]
                    if bfw(("tpqV", bank)):
                        w.append(bfw(("tpqV", bank)))
                op("tensor", "transpose", pv, stagf[j % 2][:, c * 128:(c + 1) * 128],
                   IDF[:], waits=w, inc=(sP.inc() if ci == 3 else None))
            op("vector", "tensor_copy",
               XFM3[:, cg * 4:(cg + 1) * 4, j * 128:(j + 1) * 128],
               psf(bank).rearrange("p (k t) -> p k t", t=128),
               waits=[sP.at()], inc=sV.inc())
            bank_free[("tpqV", bank)] = sV.at()
        op("scalar", "copy",
           xqb.rearrange("p (k t) -> p k t", t=1024)[:, :, j * 128:(j + 1) * 128],
           XFM3[:, :, j * 128:(j + 1) * 128], waits=[sV.at()], inc=sS.inc())
        tile_tp_done[("xq", j)] = sP.at()
    xq_done = sS.at()

    if phases <= 1:
        return _finish_debug()

    for m in range(4):
        for n in range(2):
            bank = 2 + n
            for k in range(16):
                w = []
                if k == 0:
                    w = [xq_done]
                    if m == 0 and n == 0:
                        w.append(wqa_g)
                    if bfw(("qa", bank)):
                        w.append(bfw(("qa", bank)))
                op("tensor", "matmul", psf(bank),
                   wqa_b[:, k * 512 + m * 128:k * 512 + (m + 1) * 128],
                   xqb[:, k * 1024 + n * 512:k * 1024 + (n + 1) * 512],
                   start=(k == 0), stop=(k == 15), waits=w,
                   inc=(sP.inc() if k == 15 else None))
            op("scalar", "copy", qa_b[:, m * 1024 + n * 512:m * 1024 + (n + 1) * 512],
               psf(bank), waits=[sP.at()], inc=sS.inc())
            bank_free[("qa", bank)] = sS.at()
    qa_done = sS.at()
    phase0c_pe = sP.at()

    if phases <= 1.5:
        return _finish_debug()

    ab.reset(p0_ab)
    af.reset(p0_af)
    stagb = [ab.alloc(2048) for _ in range(2)]
    grp_b = ab.alloc(16 * 256)
    wkva_b = ab.alloc(16 * 576)
    csk_s = af.alloc(2048)
    kpe_f = af.alloc(2048, parts=64)
    ktmp = af.alloc(2048, parts=64)

    op("sync", "dma_start", csk_s, csk, waits=[tile_tp_done[("xq", 7)]], inc=sQ.inc(16))
    csk_q = sQ.at()
    op("gpsimd", "dma_start", *chunked(wkva_b, wkva, 16),
       waits=[phase0c_pe], inc=sG.inc(16))
    wkva_g = sG.at()

    last_comp_evac = None
    last_kpe_evac = None
    for g in range(8):
        for ti in range(2):
            t = 2 * g + ti
            w = [phase0c_pe, consts_g] if t == 0 else []
            if t >= 2:
                w.append(tile_tp_done[t - 2])
            op("gpsimd", "dma_start", stagb[t % 2], xkv[t * 128:(t + 1) * 128, :],
               waits=w, inc=sG.inc(16))
            tload = sG.at()
            grp3 = grp_b.rearrange("p (k t) -> p k t", t=256)
            for cg in range(2):
                bank = cg
                for ci in range(8):
                    c = cg * 8 + ci
                    pv = psb(bank, ci * 64, 128)
                    w2 = []
                    if ci == 0:
                        if cg == 0:
                            w2 = [tload]
                        if t == 0:
                            w2.append(bfw(("tpqV", bank)))
                        if bfw(("tp", bank)):
                            w2.append(bfw(("tp", bank)))
                    op("tensor", "transpose", pv, stagb[t % 2][:, c * 128:(c + 1) * 128],
                       IDB[:], waits=w2, inc=(sP.inc() if ci == 7 else None))
                op("scalar", "copy",
                   grp3[:, cg * 8:(cg + 1) * 8, ti * 128:(ti + 1) * 128],
                   psb(bank, 0, 1024).rearrange("p (k t) -> p k t", t=128),
                   waits=[sP.at()], inc=sS.inc())
                bank_free[("tp", bank)] = sS.at()
            tile_tp_done[t] = sP.at()
        grp_done = sS.at()
        for m in range(5):
            parts = 128 if m < 4 else 64
            mw = 128 if m < 4 else 64
            bank = 2 + m
            for k in range(16):
                w = []
                if k == 0:
                    w = [grp_done]
                    if g == 0 and m == 0:
                        w.append(wkva_g)
                    if g == 0 and m < 2:
                        w.append(bfw(("qa", bank)))
                    if bfw(("ckv", bank)):
                        w.append(bfw(("ckv", bank)))
                op("tensor", "matmul", psf(bank, 0, 256, parts),
                   wkva_b[:, k * 576 + m * 128: k * 576 + m * 128 + mw],
                   grp_b[:, k * 256:(k + 1) * 256],
                   start=(k == 0), stop=(k == 15), waits=w,
                   inc=(sP.inc() if k == 15 else None))
            if m < 4:
                op("scalar", "copy",
                   comp_b[:, m * 2048 + g * 256: m * 2048 + (g + 1) * 256],
                   psf(bank, 0, 256), waits=[sP.at()], inc=sS.inc())
                bank_free[("ckv", bank)] = sS.at()
                last_comp_evac = sS.at()
            else:
                op("vector", "tensor_copy", kpe_f[:, g * 256:(g + 1) * 256],
                   psf(bank, 0, 256, 64), waits=[sP.at()], inc=sV.inc())
                bank_free[("ckv", bank)] = sV.at()
                last_kpe_evac = sV.at()

    swp_w = [last_kpe_evac, consts_q, last_comp_evac]
    for i in range(4):
        w = list(swp_w) if i == 0 else []
        if i >= 2 and bfw(("ckv", i)):
            w.append(bfw(("ckv", i)))
        op("tensor", "matmul", psf(i, 0, 512, 64), SWPF[:], kpe_f[:, i * 512:(i + 1) * 512],
           start=True, stop=True, waits=w, inc=(sP.inc() if i == 3 else None))
    swp_pe = sP.at()
    op("vector", "tensor_mul", kpe_f, kpe_f, csk_s[0:64, :], waits=[swp_pe, csk_q])
    for i in range(4):
        op("vector", "tensor_mul", ktmp[:, i * 512:(i + 1) * 512],
           psf(i, 0, 512, 64), csk_s[64:128, i * 512:(i + 1) * 512])
    for i in range(4):
        op("vector", "tensor_add", kpe_b[:, i * 512:(i + 1) * 512],
           kpe_f[:, i * 512:(i + 1) * 512], ktmp[:, i * 512:(i + 1) * 512],
           inc=(sV.inc() if i == 3 else None))
    kpe_done = sV.at()
    phase0_pe = sP.at()

    if phases <= 2:
        return _finish_debug()

    ab.reset(AB_P01)
    af.reset(AF_P01)
    kf2 = ab.alloc(2 * 2048)
    v2 = ab.alloc(16 * 256)
    qn2 = ab.alloc(2 * 1024)
    qp2 = ab.alloc(2 * 1024, parts=64)
    psb_ = [ab.alloc(512) for _ in range(4)]
    ptb_ = [ab.alloc(512) for _ in range(4)]
    ofm = ab.alloc(2 * 1024)
    wqb_b = [ab.alloc(4 * 384) for _ in range(2)]
    wkvb_b = [ab.alloc(4 * 512) for _ in range(2)]
    wout_b = [ab.alloc(2 * 2048) for _ in range(2)]
    qpe_f = af.alloc(1024, parts=64)
    qtmp = af.alloc(1024, parts=64)
    oacc = af.alloc(256)
    mscr = af.alloc(512)

    def pair_weights(p, extra=()):
        bb = p % 2
        op("gpsimd", "dma_start", *chunked(wqb_b[bb], wqb[:, p * 384:(p + 1) * 384], 4),
           waits=list(extra), inc=sG.inc(16))
        op("gpsimd", "dma_start", *chunked(wkvb_b[bb], wkvb[:, p * 512:(p + 1) * 512], 4),
           inc=sG.inc(16))
        op("gpsimd", "dma_start", *chunked(wout_b[bb], wout[p * 256:(p + 1) * 256, :], 2),
           inc=sG.inc(16))
        return sG.at()

    wready = [None] * 8
    wready[0] = pair_weights(0, [phase0_pe])
    wready[1] = pair_weights(1, [phase0_pe])
    pair_pe_done = {-1: None}

    for p in range(8):
        bb = p % 2
        gen_w = [wready[p], qa_done, kpe_done]
        first_gen = True
        for hh in range(2):
            for n in range(2):
                bank = n
                for k in range(4):
                    w = []
                    if k == 0:
                        if first_gen:
                            w += gen_w
                            first_gen = False
                        if bfw(("g01", bank)):
                            w.append(bfw(("g01", bank)))
                    op("tensor", "matmul", psf(bank),
                       wqb_b[bb][:, k * 384 + hh * 192: k * 384 + hh * 192 + 128],
                       qa_b[:, k * 1024 + n * 512: k * 1024 + (n + 1) * 512],
                       start=(k == 0), stop=(k == 3), waits=w,
                       inc=(sP.inc() if k == 3 else None))
                op("scalar", "copy",
                   qn2[:, hh * 1024 + n * 512: hh * 1024 + (n + 1) * 512],
                   psf(bank), waits=[sP.at()], inc=sS.inc())
                bank_free[("g01", bank)] = sS.at()
            for n in range(2):
                bank = 2 + n
                for k in range(4):
                    w = []
                    if k == 0 and bfw(("g23", bank)):
                        w.append(bfw(("g23", bank)))
                    op("tensor", "matmul", psf(bank, 0, 512, 64),
                       wqb_b[bb][:, k * 384 + hh * 192 + 128: k * 384 + hh * 192 + 192],
                       qa_b[:, k * 1024 + n * 512: k * 1024 + (n + 1) * 512],
                       start=(k == 0), stop=(k == 3), waits=w,
                       inc=(sP.inc() if k == 3 else None))
            op("vector", "tensor_copy", qpe_f[:, 0:512], psf(2, 0, 512, 64), waits=[sP.at()])
            op("vector", "tensor_copy", qpe_f[:, 512:1024], psf(3, 0, 512, 64), inc=sV.inc())
            bank_free[("g23", 2)] = sV.at()
            bank_free[("g23", 3)] = sV.at()
            qpe_evac = sV.at()
            for n in range(2):
                bank = 4 + n
                w = [qpe_evac] if n == 0 else []
                if bfw(("g45", bank)):
                    w.append(bfw(("g45", bank)))
                if bfw(("ptb", bank)):
                    w.append(bfw(("ptb", bank)))
                op("tensor", "matmul", psf(bank, 0, 512, 64), SWPF[:],
                   qpe_f[:, n * 512:(n + 1) * 512],
                   start=True, stop=True, waits=w, inc=(sP.inc() if n == 1 else None))
            op("vector", "tensor_mul", qpe_f, qpe_f, csq_s[0:64, :], waits=[sP.at()])
            for n in range(2):
                op("vector", "tensor_mul", qtmp[:, n * 512:(n + 1) * 512],
                   psf(4 + n, 0, 512, 64),
                   csq_s[64:128, n * 512:(n + 1) * 512],
                   inc=(sV.inc() if n == 1 else None))
            for n in range(2):
                bank_free[("g45", 4 + n)] = sV.at()
            for n in range(2):
                op("vector", "tensor_add",
                   qp2[:, hh * 1024 + n * 512: hh * 1024 + (n + 1) * 512],
                   qpe_f[:, n * 512:(n + 1) * 512], qtmp[:, n * 512:(n + 1) * 512],
                   inc=(sV.inc() if n == 1 else None))
            for n in range(4):
                bank = n % 2
                for k in range(4):
                    w = []
                    if k == 0 and bfw(("g01", bank)):
                        w.append(bfw(("g01", bank)))
                    op("tensor", "matmul", psf(bank),
                       wkvb_b[bb][:, k * 512 + hh * 256: k * 512 + hh * 256 + 128],
                       comp_b[:, k * 2048 + n * 512: k * 2048 + (n + 1) * 512],
                       start=(k == 0), stop=(k == 3), waits=w,
                       inc=(sP.inc() if k == 3 else None))
                op("scalar", "copy",
                   kf2[:, hh * 2048 + n * 512: hh * 2048 + (n + 1) * 512],
                   psf(bank), waits=[sP.at()], inc=sS.inc())
                bank_free[("g01", bank)] = sS.at()
            for kt in range(16):
                bank = 2 + kt % 2
                for k in range(4):
                    w = []
                    if k == 0 and bfw(("g23", bank)):
                        w.append(bfw(("g23", bank)))
                    op("tensor", "matmul", psf(bank, 0, 128),
                       comp_b[:, k * 2048 + kt * 128: k * 2048 + (kt + 1) * 128],
                       wkvb_b[bb][:, k * 512 + hh * 256 + 128: k * 512 + hh * 256 + 256],
                       start=(k == 0), stop=(k == 3), waits=w,
                       inc=(sP.inc() if k == 3 else None))
                op("scalar", "copy",
                   v2[:, kt * 256 + hh * 128: kt * 256 + hh * 128 + 128],
                   psf(bank, 0, 128), waits=[sP.at()], inc=sS.inc())
                bank_free[("g23", bank)] = sS.at()
        gen_done_s = sS.at()
        gen_done_v = sV.at()

        units = [(hh, j) for hh in range(2) for j in range(8)]
        BIAS_, SC_, RS0_ = 6, 1, 7
        score_done = {}
        emitted = [0] * len(units)

        def emit_score(ui, c):
            hh, j = units[ui]
            w = [gen_done_s, gen_done_v] if (ui == 0 and c == 0) else []
            if bfw(("sc", c)):
                w.append(bfw(("sc", c)))
            op("tensor", "matmul", psf(c),
               qn2[:, hh * 1024 + j * 128: hh * 1024 + (j + 1) * 128],
               kf2[:, hh * 2048 + c * 512: hh * 2048 + (c + 1) * 512],
               start=True, stop=False, waits=w)
            diag = (c == KCJ[j] - 1)
            op("tensor", "matmul", psf(c),
               qp2[:, hh * 1024 + j * 128: hh * 1024 + (j + 1) * 128],
               kpe_b[:, c * 512:(c + 1) * 512],
               start=False, stop=not diag,
               inc=(None if diag else sP.inc()))
            if diag:
                op("tensor", "matmul", psf(c), IDB[:],
                   MQ[:, j * 512:(j + 1) * 512], start=False, stop=True,
                   waits=([consts_g] if (p == 0 and ui == 0) else []),
                   inc=sP.inc())
            score_done[(ui, c)] = sP.at()
            emitted[ui] = c + 1

        def ensure_scores(ui):
            if ui < len(units):
                for c in range(emitted[ui], KCJ[units[ui][1]]):
                    emit_score(ui, c)

        ensure_scores(0)
        for ui, (hh, j) in enumerate(units):
            kc = KCJ[j]
            slot = ui % 2
            avb = 6 + slot
            ensure_scores(ui)
            op("vector", "tensor_reduce", smv(T0_), PS[:, 0:kc * 512], AX.X, ALU.max,
               negate=True, waits=[score_done[(ui, kc - 1)]] + ([consts_g] if ui == 0 else []))
            op("vector", "tensor_scalar_mul", SM[:, BIAS_:BIAS_ + 1], smv(T0_),
               C_ATT, inc=sV.inc())
            vmile = sV.at()
            pexps = []
            for c in range(kc):
                sw = [vmile] if c == 0 else []
                if bfw(("psb", c)):
                    sw.append(bfw(("psb", c)))
                op("scalar", "activation", psb_[c], psf(c), AF.Exp,
                   bias=SM[:, BIAS_:BIAS_ + 1], scale=C_ATT,
                   accum_out=SM[:, RS0_ + c:RS0_ + c + 1], waits=sw, inc=sS.inc())
                pexp = sS.at()
                bank_free[("sc", c)] = pexp
                pexps.append(pexp)
                if ui + 1 < len(units) and c < KCJ[units[ui + 1][1]]:
                    emit_score(ui + 1, c)
            ptcpys = []
            for c in range(kc):
                tb = 4 + c % 2
                for i in range(4):
                    w = [pexps[c]] if i == 0 else []
                    if i == 0 and bfw(("ptb", tb)):
                        w.append(bfw(("ptb", tb)))
                    op("tensor", "transpose", psb(tb, i * 64, 128),
                       psb_[c][:, i * 128:(i + 1) * 128], IDB[:],
                       waits=w, inc=(sP.inc() if i == 3 else None))
                bank_free[("psb", c)] = sP.at()
                op("vector", "tensor_copy", ptb_[c], psb(tb, 0, 512), waits=[sP.at()],
                   inc=sV.inc())
                ptcpys.append(sV.at())
                bank_free[("ptb", tb)] = sV.at()
            for c in range(kc):
                for i in range(4):
                    w = [ptcpys[c]] if i == 0 else []
                    if i == 0 and c == 0 and bfw(("av", avb)):
                        w.append(bfw(("av", avb)))
                    op("tensor", "matmul", psf(avb, 0, 128),
                       ptb_[c][:, i * 128:(i + 1) * 128],
                       v2[:, (c * 4 + i) * 256 + hh * 128: (c * 4 + i) * 256 + hh * 128 + 128],
                       start=(c == 0 and i == 0), stop=(c == kc - 1 and i == 3), waits=w,
                       inc=(sP.inc() if (c == kc - 1 and i == 3) else None))
            avdone = sP.at()
            ensure_scores(ui + 1)
            op("scalar", "copy", SM[:, SC_:SC_ + 1], SM[:, RS0_:RS0_ + 1])
            for c in range(1, kc):
                op("scalar", "add", SM[:, SC_:SC_ + 1], SM[:, SC_:SC_ + 1],
                   SM[:, RS0_ + c:RS0_ + c + 1])
            op("scalar", "copy", SM[:, SC_ + 1:SC_ + 2], SM[:, SC_:SC_ + 1],
               inc=sS.inc())
            schain = sS.at()
            op("vector", "reciprocal", SM[:, 12:14], SM[:, SC_:SC_ + 2],
               waits=[schain], inc=sV.inc())
            vrec = sV.at()
            op("scalar", "mul", OTOK[:], psf(avb, 0, 128), SM[:, 12:13],
               waits=[vrec, avdone], inc=sS.inc())
            ow = sS.at()
            op("tensor", "transpose", psb(avb, 128, 128), OTOK[:], IDB[:], waits=[ow],
               inc=sP.inc())
            op("scalar", "copy",
               ofm[:, hh * 1024 + j * 128: hh * 1024 + (j + 1) * 128],
               psb(avb, 128, 128), waits=[sP.at()], inc=sS.inc())
            bank_free[("av", avb)] = sS.at()
        attn_units_done = sS.at()

        for m in range(16):
            for n in range(2):
                bank = (m * 2 + n) % 2
                for ch in range(2):
                    w = []
                    if ch == 0:
                        w = [attn_units_done] if (m == 0 and n == 0) else []
                        if bfw(("pr", bank)):
                            w.append(bfw(("pr", bank)))
                    op("tensor", "matmul", psf(bank),
                       wout_b[bb][:, ch * 2048 + m * 128: ch * 2048 + (m + 1) * 128],
                       ofm[:, ch * 1024 + n * 512: ch * 1024 + (n + 1) * 512],
                       start=(ch == 0), stop=(ch == 1), waits=w,
                       inc=(sP.inc() if ch == 1 else None))
                op("vector", "tensor_add",
                   XFM[:, m * 1024 + n * 512: m * 1024 + (n + 1) * 512],
                   XFM[:, m * 1024 + n * 512: m * 1024 + (n + 1) * 512], psf(bank),
                   waits=[sP.at()], inc=sV.inc())
                bank_free[("pr", bank)] = sV.at()
        bank_free[("g01", 0)] = bank_free[("pr", 0)]
        bank_free[("g01", 1)] = bank_free[("pr", 1)]
        pair_pe_done[p] = sP.at()
        if p + 2 < 8:
            wready[p + 2] = pair_weights(p + 2, [pair_pe_done[p]])
    attn_done_v = sV.at()
    attn_done_s = sS.at()
    attn_pe = sP.at()

    if phases <= 3:
        return _finish_debug()

    ab.reset(0)
    x1b = ab.alloc(16 * 1024)
    sqb = ab.alloc(16 * 1024)
    af.reset(0)
    LNR = af.alloc(3072, parts=1)
    for k in range(16):
        op("scalar", "activation", sqb[:, k * 1024:(k + 1) * 1024],
           XFM[:, k * 1024:(k + 1) * 1024], AF.Square,
           waits=([attn_done_v, attn_done_s] if k == 0 else []),
           inc=(sS.inc() if k == 15 else None))
    cast_s = sS.at()
    for n in range(2):
        for k in range(16):
            w = [attn_done_v, cast_s, ones_q, ones_g] if (n == 0 and k == 0) else []
            op("tensor", "matmul", psf(n, 0, 512, 1), ONEFC[:],
               XFM[:, k * 1024 + n * 512: k * 1024 + (n + 1) * 512],
               start=(k == 0), stop=(k == 15), waits=w)
        for k in range(16):
            op("tensor", "matmul", psf(2 + n, 0, 512, 1), ONEB[:],
               sqb[:, k * 1024 + n * 512: k * 1024 + (n + 1) * 512],
               start=(k == 0), stop=(k == 15),
               inc=(sP.inc() if (n == 1 and k == 15) else None))
    sums_pe = sP.at()
    op("scalar", "copy", LNR[:, 0:512], psf(0, 0, 512, 1), waits=[sums_pe])
    op("scalar", "copy", LNR[:, 512:1024], psf(1, 0, 512, 1))
    op("scalar", "copy", LNR[:, 1024:1536], psf(2, 0, 512, 1))
    op("scalar", "copy", LNR[:, 1536:2048], psf(3, 0, 512, 1), inc=sS.inc())
    rows_s = sS.at()
    op("vector", "tensor_scalar", LNR[:, 1024:2048], LNR[:, 1024:2048], 1.0 / D, EPS,
       ALU.mult, ALU.add, waits=[rows_s])
    op("vector", "tensor_scalar_mul", LNR[:, 0:1024], LNR[:, 0:1024], 1.0 / D,
       inc=sV.inc())
    mean_v = sV.at()
    op("scalar", "activation", LNR[:, 2048:3072], LNR[:, 0:1024], AF.Square,
       waits=[mean_v], inc=sS.inc())
    m2_s = sS.at()
    op("vector", "tensor_sub", LNR[:, 1024:2048], LNR[:, 1024:2048], LNR[:, 2048:3072],
       waits=[m2_s], inc=sV.inc())
    stats_v = sV.at()
    op("scalar", "activation", LNR[:, 1024:2048], LNR[:, 1024:2048], AF.Relu,
       waits=[stats_v])
    op("scalar", "activation", LNR[:, 1024:2048], LNR[:, 1024:2048], AF.Ln,
       bias=EPSC[0:1, 0:1])
    op("scalar", "activation", LNR[:, 1024:2048], LNR[:, 1024:2048], AF.Exp, scale=-0.5,
       inc=sS.inc())
    rstd_s = sS.at()
    op("vector", "tensor_mul", LNR[:, 0:1024], LNR[:, 0:1024], LNR[:, 1024:2048],
       waits=[rstd_s], inc=sV.inc())
    meanr_v = sV.at()
    for n in range(2):
        op("tensor", "matmul", psf(n), ONEF[:],
           LNR[0:1, 1024 + n * 512:1024 + (n + 1) * 512],
           start=True, stop=True, waits=([meanr_v] if n == 0 else []))
        op("tensor", "matmul", psf(2 + n), ONEF[:], LNR[0:1, n * 512:(n + 1) * 512],
           start=True, stop=True, inc=(sP.inc() if n == 1 else None))
    bcast_pe = sP.at()
    for k in range(16):
        for n in range(2):
            sl = slice(k * 1024 + n * 512, k * 1024 + (n + 1) * 512)
            w = [bcast_pe] if (k == 0 and n == 0) else []
            op("vector", "tensor_mul", XFM[:, sl], XFM[:, sl], psf(n), waits=w)
            op("vector", "tensor_sub", XFM[:, sl], XFM[:, sl], psf(2 + n))
            op("vector", "tensor_copy", x1b[:, sl], XFM[:, sl],
               inc=(sV.inc() if (k == 15 and n == 1) else None))
    ln1_done = sV.at()

    if phases <= 4:
        return _finish_debug()

    ab.reset(16 * 1024)
    af.reset(5120)
    rg_f = af.alloc(2048, parts=8)
    rT_f = af.alloc(128)
    af.reset(0)
    lg_f = af.alloc(1024, parts=8)
    GFM = af.alloc(1024, parts=8)
    af_router = af.mark()
    ltok = af.alloc(64)
    gtok = af.alloc(64)
    sigt = af.alloc(64)
    op("sync", "dma_start", rg_f, rgate, waits=[attn_pe], inc=sQ.inc(16))
    rg_g = sQ.at()
    for kg in range(2):
        bank = 4 + kg
        for ki in range(8):
            k = kg * 8 + ki
            w = [rg_g, bfw(("ptb", 4)), bfw(("ptb", 5))] if k == 0 else []
            op("tensor", "transpose", psf(bank, ki * 8, 8), rg_f[:, k * 128:(k + 1) * 128],
               IDF[0:8, 0:8], waits=w, inc=(sP.inc() if ki == 7 else None))
        op("scalar", "copy", rT_f[:, kg * 64:(kg + 1) * 64], psf(bank, 0, 64),
           waits=[sP.at()], inc=sS.inc())
    rT_done = sS.at()
    for n in range(2):
        for k in range(16):
            w = [rT_done, ln1_done] if (n == 0 and k == 0) else []
            op("tensor", "matmul", psf(n, 0, 512, 8), rT_f[:, k * 8:(k + 1) * 8],
               XFM[:, k * 1024 + n * 512: k * 1024 + (n + 1) * 512],
               start=(k == 0), stop=(k == 15), waits=w,
               inc=(sP.inc() if k == 15 else None))
    lg_pe = sP.at()
    op("vector", "tensor_copy", lg_f[:, 0:512], psf(0, 0, 512, 8), waits=[lg_pe])
    op("vector", "tensor_copy", lg_f[:, 512:1024], psf(1, 0, 512, 8), inc=sV.inc())
    lg_v = sV.at()
    for tt in range(8):
        w = [lg_v] if tt == 0 else []
        op("tensor", "transpose", psf(2, tt * 8, 8), lg_f[:, tt * 128:(tt + 1) * 128],
           IDF[0:8, 0:8], waits=w, inc=(sP.inc() if tt == 7 else None))
    op("vector", "tensor_copy", ltok, psf(2, 0, 64), waits=[sP.at()], inc=sV.inc())
    V1C, V2C = 16, 24
    for tt in range(8):
        op("vector", "reduce_max", SM[:, V1C + tt:V1C + tt + 1],
           ltok[:, tt * 8:(tt + 1) * 8], AX.X)
    for tt in range(8):
        op("vector", "tensor_scalar", gtok[:, tt * 8:(tt + 1) * 8],
           ltok[:, tt * 8:(tt + 1) * 8], SM[:, V1C + tt:V1C + tt + 1], NEG,
           ALU.is_equal, ALU.mult)
    for tt in range(8):
        op("vector", "tensor_add", gsc, ltok[:, tt * 8:(tt + 1) * 8],
           gtok[:, tt * 8:(tt + 1) * 8]) if False else None
    for tt in range(8):
        op("vector", "tensor_add", gtok[:, tt * 8:(tt + 1) * 8],
           gtok[:, tt * 8:(tt + 1) * 8], ltok[:, tt * 8:(tt + 1) * 8])
    for tt in range(8):
        op("vector", "reduce_max", SM[:, V2C + tt:V2C + tt + 1],
           gtok[:, tt * 8:(tt + 1) * 8], AX.X,
           inc=(sV.inc() if tt == 7 else None))
    vtop = sV.at()
    for tt in range(8):
        op("scalar", "add", SM[:, V1C + tt:V1C + tt + 1], SM[:, V1C + tt:V1C + tt + 1],
           SM[:, V2C + tt:V2C + tt + 1], waits=([vtop] if tt == 0 else []))
        op("scalar", "mul", SM[:, V1C + tt:V1C + tt + 1], SM[:, V1C + tt:V1C + tt + 1],
           -1.0, inc=(sS.inc() if tt == 7 else None))
    s12done = sS.at()
    for tt in range(8):
        op("vector", "tensor_scalar", gtok[:, tt * 8:(tt + 1) * 8],
           ltok[:, tt * 8:(tt + 1) * 8], 2.0, SM[:, V1C + tt:V1C + tt + 1],
           ALU.mult, ALU.add, waits=([s12done] if tt == 0 else []),
           inc=(sV.inc() if tt == 7 else None))
    zdone = sV.at()
    op("scalar", "activation", sigt, gtok[:, 0:64], AF.Sigmoid, waits=[zdone],
       inc=sS.inc())
    sigdone = sS.at()
    for tt in range(8):
        op("vector", "tensor_scalar", gtok[:, tt * 8:(tt + 1) * 8],
           ltok[:, tt * 8:(tt + 1) * 8], SM[:, V2C + tt:V2C + tt + 1], None,
           ALU.is_ge, waits=([sigdone] if tt == 0 else []))
    for tt in range(8):
        op("vector", "tensor_mul", gtok[:, tt * 8:(tt + 1) * 8],
           gtok[:, tt * 8:(tt + 1) * 8], sigt[:, tt * 8:(tt + 1) * 8],
           inc=(sV.inc() if tt == 7 else None))
    gtok_v = sV.at()

    for tt in range(8):
        bank = 3 + tt // 4
        dst = PS[0:8, bank * 512 + (tt % 4) * 128: bank * 512 + (tt % 4) * 128 + 128]
        w = [gtok_v] if tt == 0 else []
        op("tensor", "transpose", dst, gtok[:, tt * 8:(tt + 1) * 8], IDF[:],
           waits=w, inc=(sP.inc() if tt == 7 else None))
    op("vector", "tensor_copy", GFM, PS[0:8, 3 * 512: 3 * 512 + 1024],
       waits=[sP.at()], inc=sV.inc())
    router_done = sV.at()
    router_pe = sP.at()

    if phases <= 5:
        return _finish_debug()

    ab.reset(16 * 1024)
    wgu_b = [ab.alloc(16 * 256) for _ in range(2)]
    wdn_b = ab.alloc(8 * 2048)
    h_b = ab.alloc(8 * 1024)
    af.reset(af_router)
    GROW = af.alloc(1024, parts=1)
    gbc = af.alloc(1024)
    silu_s = af.alloc(1024)

    def moe_wgu(e, q8, extra=()):
        op("gpsimd", "dma_start", *chunked(wgu_b[q8 % 2], wgu[e, :, q8 * 256:(q8 + 1) * 256], 16),
           waits=list(extra), inc=sG.inc(16))
        return sG.at()

    wg_ready = {(0, 0): moe_wgu(0, 0, [router_done, router_pe]),
                (0, 1): moe_wgu(0, 1)}
    moe_pe_after = router_pe
    gu_pe_prev = router_pe

    for e in range(NE):
        wd_ready = None
        op("sync", "dma_start", GROW, GFM[e:e + 1, :],
           waits=[router_done] + ([bank_free.get(("grow",))] if bank_free.get(("grow",)) else []),
           inc=sQ.inc(16))
        grow_q = sQ.at()
        for n in range(2):
            bank = 6 + n
            w = [grow_q] if n == 0 else []
            if bfw(("gb", bank)):
                w.append(bfw(("gb", bank)))
            op("tensor", "matmul", psf(bank), ONEF[:], GROW[:, n * 512:(n + 1) * 512],
               start=True, stop=True, waits=w, inc=(sP.inc() if n == 1 else None))
        bank_free[("grow",)] = sP.at()
        op("vector", "tensor_copy", gbc[:, 0:512], psf(6), waits=[sP.at()])
        op("vector", "tensor_copy", gbc[:, 512:1024], psf(7), inc=sV.inc())
        bank_free[("gb", 6)] = sV.at()
        bank_free[("gb", 7)] = sV.at()
        gb_done = sV.at()
        for q8 in range(8):
            nxt = (e, q8 + 1) if q8 + 1 < 8 else (e + 1, 0)
            if nxt[0] < NE and nxt not in wg_ready:
                wg_ready[nxt] = moe_wgu(nxt[0], nxt[1], [gu_pe_prev])
            if q8 == 2:
                op("gpsimd", "dma_start", *chunked(wdn_b, wdn[e], 8),
                   waits=[moe_pe_after], inc=sG.inc(16))
                wd_ready = sG.at()
            wrd = wg_ready.pop((e, q8))
            for n in range(2):
                for k in range(16):
                    w = []
                    if k == 0:
                        w = [wrd]
                        if e == 0 and q8 == 0 and n == 0:
                            w.append(ln1_done)
                        if bfw(("gu", n)):
                            w.append(bfw(("gu", n)))
                        if bfw(("gu", 2 + n)):
                            w.append(bfw(("gu", 2 + n)))
                    op("tensor", "matmul", psf(n),
                       wgu_b[q8 % 2][:, k * 256: k * 256 + 256: 2],
                       x1b[:, k * 1024 + n * 512: k * 1024 + (n + 1) * 512],
                       start=(k == 0), stop=(k == 15), waits=w)
                    op("tensor", "matmul", psf(2 + n),
                       wgu_b[q8 % 2][:, k * 256 + 1: k * 256 + 256: 2],
                       x1b[:, k * 1024 + n * 512: k * 1024 + (n + 1) * 512],
                       start=(k == 0), stop=(k == 15),
                       inc=(sP.inc() if (n == 1 and k == 15) else None))
            gu_pe = sP.at()
            gu_pe_prev = gu_pe
            for n in range(2):
                sl = slice(n * 512, (n + 1) * 512)
                w = [gu_pe, gb_done] if n == 0 else []
                op("scalar", "activation", silu_s[:, sl], psf(n), AF.Silu, waits=w,
                   inc=sS.inc())
                op("vector", "tensor_mul", silu_s[:, sl], silu_s[:, sl], psf(2 + n),
                   waits=[sS.at()])
                op("vector", "tensor_mul",
                   h_b[:, q8 * 1024 + n * 512: q8 * 1024 + (n + 1) * 512],
                   silu_s[:, sl], gbc[:, sl], inc=sV.inc())
                bank_free[("gu", n)] = sV.at()
                bank_free[("gu", 2 + n)] = sV.at()
        gu_done = sV.at()
        for m in range(16):
            for n in range(2):
                bank = 4 + (m * 2 + n) % 2
                for i in range(8):
                    w = []
                    if i == 0:
                        w = [wd_ready, gu_done] if (m == 0 and n == 0) else []
                        if bfw(("dn", bank)):
                            w.append(bfw(("dn", bank)))
                    op("tensor", "matmul", psf(bank),
                       wdn_b[:, i * 2048 + m * 128: i * 2048 + (m + 1) * 128],
                       h_b[:, i * 1024 + n * 512: i * 1024 + (n + 1) * 512],
                       start=(i == 0), stop=(i == 7), waits=w,
                       inc=(sP.inc() if i == 7 else None))
                op("vector", "tensor_add",
                   XFM[:, m * 1024 + n * 512: m * 1024 + (n + 1) * 512],
                   XFM[:, m * 1024 + n * 512: m * 1024 + (n + 1) * 512], psf(bank),
                   waits=[sP.at()], inc=sV.inc())
                bank_free[("dn", bank)] = sV.at()
        moe_pe_after = sP.at()
    moe_done_v = sV.at()

    if phases <= 6:
        return _finish_debug()

    af.reset(0)
    ost = [af.alloc(2048) for _ in range(2)]
    osq = af.alloc(2048)
    for j in range(8):
        bb = j % 2
        for cg in range(4):
            bank = cg % 2
            for ci in range(4):
                c = cg * 4 + ci
                pv = psf(bank, ci * 128, 128)
                w = []
                if ci == 0:
                    if j == 0 and cg == 0:
                        w.append(moe_done_v)
                    if bfw(("o", bank)):
                        w.append(bfw(("o", bank)))
                op("tensor", "transpose", pv,
                   XFM[:, c * 1024 + j * 128: c * 1024 + (j + 1) * 128],
                   IDF[:], waits=w, inc=(sP.inc() if ci == 3 else None))
            cw = [sP.at()]
            if cg == 0 and j >= 2:
                cw.append((sO.sem, (j - 1) * 16))
            op("scalar", "copy", ost[bb][:, cg * 512:(cg + 1) * 512], psf(bank),
               waits=cw, inc=sS.inc())
            bank_free[("o", bank)] = sS.at()
        tile_s = sS.at()
        op("vector", "reduce_sum", smv(T0_), ost[bb], AX.X, waits=[tile_s],
           inc=sV.inc())
        sq_w = [tile_s]
        if j > 0:
            sq_w.append(bank_free.get(("osq",)))
        op("scalar", "activation", osq, ost[bb], AF.Square, waits=sq_w, inc=sS.inc())
        op("vector", "reduce_sum", smv(T1_), osq, AX.X, waits=[sS.at()], inc=sV.inc())
        bank_free[("osq",)] = sV.at()
        op("scalar", "mul", smv(T0_), smv(T0_), 1.0 / D, waits=[sV.at()])
        op("scalar", "activation", smv(T2_), smv(T0_), AF.Square)
        op("scalar", "mul", smv(T2_), smv(T2_), -1.0)
        op("scalar", "mul", smv(T1_), smv(T1_), 1.0 / D)
        op("scalar", "add", smv(T1_), smv(T1_), smv(T2_))
        op("scalar", "activation", smv(T1_), smv(T1_), AF.Relu)
        op("scalar", "activation", smv(T1_), smv(T1_), AF.Ln, bias=EPSC[:, 0:1])
        op("scalar", "activation", smv(T1_), smv(T1_), AF.Exp, scale=-0.5, inc=sS.inc())
        op("vector", "tensor_scalar", ost[bb], ost[bb], smv(T0_), smv(T1_),
           ALU.subtract, ALU.mult, waits=[sS.at()], inc=sV.inc())
        op("sync", "dma_start", out[j * 128:(j + 1) * 128, :], ost[bb],
           waits=[sV.at()], inc=sO.inc(16))
    waitonly("sync", sO.at())

    _emit_all()
    es.close()
    return nc



def _rope_tables(positions, n):
    r = np.arange(ROPE // 2, dtype=np.float64)
    theta = 10000.0 ** (-2.0 * r / ROPE)
    ang = positions[None, :].astype(np.float64) * theta[:, None]
    C = np.empty((64, n), dtype=np.float64)
    S = np.empty((64, n), dtype=np.float64)
    C[0::2] = np.cos(ang)
    C[1::2] = np.cos(ang)
    S[0::2] = -np.sin(ang)
    S[1::2] = np.sin(ang)
    return np.concatenate([C, S], axis=0).astype(np.float32)


def kernel(L0_attn_x, L0_attn_Wqa, L0_attn_Wqb, L0_attn_Wkva, L0_attn_Wkvb,
           L0_attn_Wout, L0_ffn_router_gate, L0_ffn_expert_gate_up,
           L0_ffn_expert_down):
    x = np.ascontiguousarray(np.asarray(L0_attn_x, dtype=np.float32))
    wqa = np.ascontiguousarray(np.asarray(L0_attn_Wqa, dtype=np.float32))
    wqb = np.ascontiguousarray(np.asarray(L0_attn_Wqb, dtype=np.float32))
    wkva = np.ascontiguousarray(np.asarray(L0_attn_Wkva, dtype=np.float32))
    wkvb = np.ascontiguousarray(np.asarray(L0_attn_Wkvb, dtype=np.float32))
    wout = np.ascontiguousarray(np.asarray(L0_attn_Wout, dtype=np.float32))
    rg = np.ascontiguousarray(np.asarray(L0_ffn_router_gate, dtype=np.float32))
    wgu = np.ascontiguousarray(np.asarray(L0_ffn_expert_gate_up, dtype=np.float32))
    wdn = np.ascontiguousarray(np.asarray(L0_ffn_expert_down, dtype=np.float32))

    ident = np.eye(128, dtype=np.float32)
    swp = np.zeros((64, 64), dtype=np.float32)
    for i in range(32):
        swp[2 * i, 2 * i + 1] = 1.0
        swp[2 * i + 1, 2 * i] = 1.0
    csk = _rope_tables(np.arange(NK), NK)

    in_maps = []
    for core in range(8):
        b, h = core // 2, core % 2
        qs = QSETS[h]
        rows = np.concatenate([np.arange(q * 128, (q + 1) * 128) for q in qs])
        xq_h = np.ascontiguousarray(x[b][rows])
        mask = np.empty((128, 8 * 512), dtype=np.float32)
        for j, q in enumerate(qs):
            kc = KCJ[j]
            rr = q * 128 + np.arange(128)[:, None]
            cc = (kc - 1) * 512 + np.arange(512)[None, :]
            mask[:, j * 512:(j + 1) * 512] = np.where(cc <= rr, 0.0, -65536.0)
        csq_h = _rope_tables(rows, NQ)
        in_maps.append({
            "xq": xq_h, "xkv": x[b], "wqa": wqa, "wqb": wqb, "wkva": wkva,
            "wkvb": wkvb, "wout": wout, "rgate": rg, "wgu": wgu, "wdn": wdn,
            "maskin": mask, "csq": csq_h, "csk": csk, "identin": ident,
            "swpin": swp, "onesrow": np.ones((1, 128), np.float32),
            "onescol": np.ones((128, 1), np.float32),
            "epscol": np.full((128, 1), 1e-5, np.float32),
        })

    nc = build_program(PHASES)
    res = run_bass_kernel_spmd(nc, in_maps, core_ids=list(range(8)))

    outp = np.empty((4, 2048, 2048), dtype=np.float32)
    for core in range(8):
        b, h = core // 2, core % 2
        o = np.asarray(res.results[core]["out"], dtype=np.float32)
        for j, q in enumerate(QSETS[h]):
            outp[b, q * 128:(q + 1) * 128, :] = o[j * 128:(j + 1) * 128, :]
    return outp
